# version 06
# baseline (speedup 1.0000x reference)
import contextlib
import numpy as np
import ml_dtypes
import concourse.bass as bass
import concourse.mybir as mybir
from concourse.bass_utils import run_bass_kernel_spmd

F32 = mybir.dt.float32
BF16 = mybir.dt.bfloat16
AF = mybir.ActivationFunctionType
ALU = mybir.AluOpType
AX = mybir.AxisListType.X

NC = 8
TOK = 1024
D = 2048
DCH = D // 128
HID = 5632
HCH = HID // 128
H = 512
RG = [[0, 1, 2, 3], [4, 5, 6, 7]]
RG8 = [[0, 1, 2, 3, 4, 5, 6, 7]]
GLA_HEADS = 16
GLA_CH = 16
DBG_TAP = None
LN_EPS = 1e-5
RMS_EPS = 1e-6
QEPS = 1e-8
QW = 1408

ENG_ATTR = {"t": "tensor", "v": "vector", "s": "scalar", "g": "gpsimd", "y": "sync"}


class DSem:
    def __init__(self, handle):
        self.h = handle
        self.cnt = 0


class Builder:
    def __init__(self):
        self.nc = bass.Bass("TRN2", target_bir_lowering=False)
        self.es = contextlib.ExitStack()
        self.phases = []
        self.sem = {}
        self.cnt = {}
        self.waited = {}
        self.dsems = {}
        for e in ENG_ATTR:
            self.sem[e] = self.es.enter_context(self.nc.semaphore(f"S{e}"))
            self.cnt[e] = 0
        self.cur = None
        self.phase("init")

    def dsem(self, name):
        if name not in self.dsems:
            self.dsems[name] = DSem(self.es.enter_context(self.nc.semaphore(f"D{name}")))
        return self.dsems[name]

    def phase(self, name):
        self.cur = {e: [] for e in ENG_ATTR}
        self.cur["name"] = name
        self.phases.append(self.cur)

    def sb(self, name, shape, dt=F32):
        return self.es.enter_context(self.nc.sbuf_tensor(name, shape, dt))

    def ps(self, name, shape, dt=F32):
        return self.es.enter_context(self.nc.psum_tensor(name, shape, dt))

    def dram(self, name, shape, dt=F32, kind="Internal"):
        return self.nc.dram_tensor(name, shape, dt, kind=kind)

    def op(self, eng, maker, inc=True):
        sem = self.sem[eng] if inc else None
        if inc:
            self.cnt[eng] += 1
        tgt = self.cnt[eng]

        def fn(e, maker=maker, sem=sem):
            inst = maker(e)
            if sem is not None:
                inst.then_inc(sem, 1)
        self.cur[eng].append(fn)
        return (self.sem[eng], tgt)

    def tok(self, eng):
        return (self.sem[eng], self.cnt[eng])

    def raw(self, eng, fn):
        self.cur[eng].append(fn)

    def wait(self, eng, tok):
        if tok is None:
            return
        sem, v = tok
        if v <= 0:
            return
        key = (eng, sem)
        if self.waited.get(key, 0) >= v:
            return
        self.waited[key] = v

        def fn(e, sem=sem, v=v):
            e.wait_ge(sem, v)
        self.cur[eng].append(fn)

    def waits(self, eng, toks):
        for t in toks:
            self.wait(eng, t)

    def soft_fence(self, engines=("t", "v", "s", "g", "y")):
        for e in engines:
            for p in ("t", "v", "s"):
                self.wait(e, (self.sem[p], self.cnt[p]))

    def dma_fence(self):
        for e in ENG_ATTR:
            for ds in self.dsems.values():
                self.wait(e, (ds.h, ds.cnt))

    def dma(self, queue, ds, out, in_, **kw):
        ds.cnt += 16
        tgt = ds.cnt
        sem = ds.h

        def fn(e, out=out, in_=in_, kw=kw, sem=sem):
            e.dma_start(out=out, in_=in_, **kw).then_inc(sem, 16)
        self.cur[queue].append(fn)
        return (sem, tgt)


def _finalize(b):
    nc = b.nc
    for ph in b.phases:
        with nc.Block(ph["name"]) as blk:
            for ename, attr in ENG_ATTR.items():
                ops = ph[ename]
                if not ops:
                    continue
                getattr(blk, attr)(lambda e, ops=ops: [f(e) for f in ops])
    b.es.close()
    return nc


class Arena:
    def __init__(self, b, name, els, dt):
        self.t = b.sb(name, [128, els], dt)
        self.els = els
        self.off = 0
        self.marks = []
        self.peak = 0

    def alloc(self, els):
        a = self.off
        self.off += els
        self.peak = max(self.peak, self.off)
        assert self.off <= self.els, f"arena {self.t.name} overflow {self.off}>{self.els}"
        return self.t[:, a:a + els]

    def mark(self):
        self.marks.append(self.off)

    def release(self):
        self.off = self.marks.pop()

    def reset(self):
        self.off = 0
        self.marks = []


class Ring:
    _uid = 0

    def __init__(self, b, bufs, name=None):
        self.b = b
        self.bufs = bufs
        self.n = len(bufs)
        self.i = 0
        self.free = [[] for _ in range(self.n)]
        self.loaded = [None] * self.n
        if name is None:
            name = f"ring{Ring._uid}"
            Ring._uid += 1
        self.ds = [b.dsem(f"{name}_{j}") for j in range(self.n)]

    def load(self, queue, in_, dstview=None, **kw):
        j = self.i % self.n
        self.i += 1
        for t in self.free[j]:
            self.b.wait(queue, t)
        self.free[j] = []
        dst = self.bufs[j] if dstview is None else dstview(self.bufs[j])
        self.loaded[j] = self.b.dma(queue, self.ds[j], dst, in_, **kw)
        return j

    def used(self, j, tok):
        self.free[j].append(tok)


class Banks:
    def __init__(self, b, handles):
        self.b = b
        self.h = handles
        self.n = len(handles)
        self.i = 0
        self.free = [None] * self.n

    def acquire(self):
        for _ in range(self.n):
            j = self.i % self.n
            self.i += 1
            if j in getattr(self, "taken", ()):
                continue
            self.b.wait("t", self.free[j])
            return j
        raise RuntimeError("no free banks")

    def release(self, j, tok):
        self.free[j] = tok

    def take(self, j):
        if not hasattr(self, "taken"):
            self.taken = set()
        self.b.wait("t", self.free[j])
        self.taken.add(j)

    def untake(self, j):
        self.taken.discard(j)


def cpart(buf, c, t0, tn):
    return buf[:, c * TOK + t0: c * TOK + t0 + tn]



def emit_layernorm(b, C, src, dst, src_tok):
    pb, banks = C["pb"], C["banks"]
    abf, af32 = C["abf"], C["af32"]
    abf.mark(); af32.mark()
    sq = [abf.alloc(TOK) for _ in range(2)]
    tsc = [abf.alloc(H) for _ in range(2)]
    rows = af32.alloc(6 * H)
    mu, e2, t1, t2, rr, ss = (rows[0:1, i * H:(i + 1) * H] for i in range(6))
    b.wait("s", src_tok)
    bS = [banks.acquire(), banks.acquire()]
    bQ = [banks.acquire(), banks.acquire()]
    sq_gate = [None] * 2
    for c in range(DCH):
        ct = b.op("s", lambda e, c=c: e.copy(cpart(dst, c, 0, TOK), cpart(src, c, 0, TOK)))
        b.wait("s", ct)
        b.wait("s", sq_gate[c % 2])
        st = b.op("s", lambda e, c=c: e.activation(sq[c % 2], cpart(dst, c, 0, TOK), AF.Square))
        b.wait("t", ct)
        for hf in (0, 1):
            b.op("t", lambda e, c=c, hf=hf: e.matmul(
                pb[bS[hf]][0:1, :], C["ones128_bf"][:, :], cpart(dst, c, hf * H, H),
                start=(c == 0), stop=(c == DCH - 1)), inc=False)
        b.wait("t", st)
        for hf in (0, 1):
            mt = b.op("t", lambda e, c=c, hf=hf: e.matmul(
                pb[bQ[hf]][0:1, :], C["ones128_bf"][:, :], sq[c % 2][:, hf * H:(hf + 1) * H],
                start=(c == 0), stop=(c == DCH - 1)), inc=True)
        sq_gate[c % 2] = mt
    stat_tok = b.tok("t")
    out_tok = None
    bb = []
    for hf in (0, 1):
        b.wait("v", stat_tok)
        b.op("v", lambda e, hf=hf: e.tensor_scalar(mu, pb[bS[hf]][0:1, :], 1.0 / D, None, ALU.mult))
        b.op("v", lambda e, hf=hf: e.tensor_scalar(e2, pb[bQ[hf]][0:1, :], 1.0 / D, None, ALU.mult))
        b.wait("v", b.tok("v"))
        b.op("v", lambda e: e.tensor_tensor(t1, mu, mu, ALU.mult))
        b.wait("v", b.tok("v"))
        b.op("v", lambda e: e.scalar_tensor_tensor(t2, e2, 1.0, t1, ALU.mult, ALU.subtract))
        b.wait("v", b.tok("v"))
        vtok = b.op("v", lambda e: e.tensor_scalar(t2, t2, LN_EPS, None, ALU.add))
        b.wait("s", vtok)
        st = b.op("s", lambda e: e.activation(t1, t2, AF.Ln))
        b.wait("s", st)
        st = b.op("s", lambda e: e.activation(rr, t1, AF.Exp, scale=-0.5))
        b.wait("v", st)
        vtok = b.op("v", lambda e: e.tensor_tensor(ss, mu, rr, ALU.mult))
        banks.release(bS[hf], vtok)
        bR = banks.acquire()
        bT = bQ[hf]
        b.wait("t", vtok)
        b.op("t", lambda e, bR=bR: e.matmul(pb[bR][:, :], C["ones1x128_f32"][:, :], rr, start=True, stop=True))
        btok = b.op("t", lambda e, bT=bT: e.matmul(pb[bT][0:128, :], C["ones1x128_f32"][:, :], ss, start=True, stop=True))
        b.wait("v", btok)
        tgate = [None, None]
        for c in range(DCH):
            b.wait("v", tgate[c % 2])
            mtok = b.op("v", lambda e, c=c, hf=hf, bR=bR: e.tensor_tensor(
                tsc[c % 2], cpart(dst, c, hf * H, H), pb[bR][:, :], ALU.mult))
            b.wait("v", mtok)
            out_tok = b.op("v", lambda e, c=c, hf=hf, bT=bT: e.tensor_tensor(
                cpart(dst, c, hf * H, H), tsc[c % 2], pb[bT][0:128, :], ALU.subtract))
            tgate[c % 2] = out_tok
        banks.release(bR, out_tok)
        bb.append((bT, out_tok))
    for bT, tk in bb:
        banks.release(bT, tk)
    abf.release(); af32.release()
    return out_tok


def emit_quantize_coop(b, C, qs, jobs):
    pb, banks = C["pb"], C["banks"]
    N = len(RG8[0])
    rankcell = {}

    def _rank(e):
        if id(e) not in rankcell:
            rankcell[id(e)] = e.cc_rank(RG8)
        return rankcell[id(e)]
    ring = qs["ring"]
    asums = qs["asums"]
    qa, qb = qs["qa"], qs["qb"]
    dmy = qs["std"]
    nj = len(jobs)
    ptot = qs["ptot"]
    b.op("v", lambda e: e.memset(ptot, 0.0))
    red_tok = None
    ntg = 0
    for ji, (wt_d, wq_d) in enumerate(jobs):
        nel = int(wt_d.shape[0]) * int(wt_d.shape[1])
        sl = nel // N
        nch = (sl + 128 * QW - 1) // (128 * QW)
        wflat = wt_d.ap().opt().rearrange("(r k p x) -> r k p x", r=N, k=nch, p=128, x=sl // (128 * nch))
        xw = sl // (128 * nch)
        nt = 0
        for kk in range(nch):
            def ld(e, wflat=wflat, kk=kk, dst=None):
                pass
            j = ring.i % ring.n
            for t_ in ring.free[j]:
                b.wait("g", t_)
            ring.free[j] = []
            ring.i += 1
            dsem = ring.ds[j]
            dsem.cnt += 16
            tokv = (dsem.h, dsem.cnt)
            wk = wflat[:, kk, :, :]
            def fn(e, wk=wk, dst=ring.bufs[j][:, 0:xw], sem=dsem.h):
                rank = _rank(e)
                e.dma_start(out=dst, in_=wk[bass.ds(rank, 1), :, :]).then_inc(sem, 16)
            b.cur["g"].append(fn)
            ring.loaded[j] = tokv
            b.wait("v", tokv)
            vt = b.op("v", lambda e, j=j, c0=ntg + nt, xw=xw: e.tensor_reduce(
                asums[:, c0:c0 + 1], ring.bufs[j][:, 0:xw], AX, ALU.add,
                apply_absolute_value=True))
            ring.used(j, vt)
            b.wait("t", vt)
            for _ in range(2):
                b.op("t", lambda e: e.matmul(C["pbc"][0:1, 0:1], C["ones128_bf"][:, :],
                                             C["ones128_bf"][:, :], start=True, stop=True), inc=False)
            nt += 1
        b.wait("v", b.tok("v"))
        vt = b.op("v", lambda e, c0=ntg, nt=nt: e.tensor_reduce(
            C["q_tot"], asums[:, c0:c0 + nt], AX, ALU.add))
        ntg += nt
        bk = banks.acquire()
        b.wait("t", vt)
        mt = b.op("t", lambda e, bk=bk, ji=ji: e.matmul(pb[bk][0:1, ji:ji + 1], C["ones128_f32"][:, :], C["q_tot"], start=True, stop=True))
        b.wait("v", mt)
        red_tok = b.op("v", lambda e, bk=bk, ji=ji: e.tensor_copy(ptot[0:1, ji:ji + 1], pb[bk][0:1, ji:ji + 1]))
        banks.release(bk, red_tok)
    b.wait("g", red_tok)
    ar_in, ar_out = C["qar_in"], C["qar_out"]
    dq = b.dsem("qcoop")
    t_ = b.dma("g", dq, ar_in[:, :], ptot[0:1, 0:8])
    b.wait("g", t_)
    cc = b.op("g", lambda e: e.collective_compute(
        "AllReduce", ALU.add, replica_groups=RG8, ins=[ar_in.ap()], outs=[ar_out.ap()]))
    b.wait("g", cc)
    t_ = b.dma("g", dq, ptot[0:1, 0:8], ar_out[:, :])
    b.wait("v", t_)
    for ji, (wt_d, wq_d) in enumerate(jobs):
        nel = float(int(wt_d.shape[0]) * int(wt_d.shape[1]))
        vt = b.op("v", lambda e, ji=ji, nel=nel: e.tensor_scalar(
            C["q_thr1"], ptot[0:1, ji:ji + 1], 0.5 / nel, 0.5 * QEPS, ALU.mult, ALU.add))
        bk = banks.acquire()
        b.wait("t", vt)
        mt = b.op("t", lambda e, bk=bk: e.matmul(pb[bk][:, 0:1], C["ones1x128_f32"][:, :], C["q_thr1"], start=True, stop=True))
        b.wait("v", mt)
        b.wait("v", b.tok("v"))
        vt = b.op("v", lambda e, bk=bk, ji=ji: e.tensor_copy(C["q_thrc"][:, ji:ji + 1], pb[bk][:, 0:1]))
        banks.release(bk, vt)
        b.wait("v", vt)
        thr_tok = b.op("v", lambda e, ji=ji: e.tensor_scalar(
            C["q_nthr"][:, ji:ji + 1], C["q_thrc"][:, ji:ji + 1], -1.0, None, ALU.mult))
    mk_gate = None
    ag_toks = []
    for ji, (wt_d, wq_d) in enumerate(jobs):
        nel = int(wt_d.shape[0]) * int(wt_d.shape[1])
        sl = nel // N
        nch = (sl + 128 * QW - 1) // (128 * QW)
        xw = sl // (128 * nch)
        wflat = wt_d.ap().opt().rearrange("(r k p x) -> r k p x", r=N, k=nch, p=128, x=xw)
        wb_d = C[f"qb{ji}_d"]
        bflat = wb_d.ap().rearrange("(k p) x -> k p x", p=128)
        for kk in range(nch):
            j = ring.i % ring.n
            for t2 in ring.free[j]:
                b.wait("g", t2)
            ring.free[j] = []
            ring.i += 1
            dsem = ring.ds[j]
            dsem.cnt += 16
            tokv = (dsem.h, dsem.cnt)
            wk = wflat[:, kk, :, :]
            def fn(e, wk=wk, dst=ring.bufs[j][:, 0:xw], sem=dsem.h):
                rank = _rank(e)
                e.dma_start(out=dst, in_=wk[bass.ds(rank, 1), :, :]).then_inc(sem, 16)
            b.cur["g"].append(fn)
            ring.loaded[j] = tokv
            b.wait("v", tokv)
            b.wait("v", thr_tok)
            b.wait("v", mk_gate)
            b.op("v", lambda e, j=j, ji=ji, xw=xw: e.tensor_scalar(
                qa[:, 0:xw], ring.bufs[j][:, 0:xw], C["q_thrc"][:, ji:ji + 1], None, ALU.is_ge))
            b.wait("t", b.tok("v"))
            for _ in range(2):
                b.op("t", lambda e: e.matmul(C["pbc"][0:1, 0:1], C["ones128_bf"][:, :],
                                             C["ones128_bf"][:, :], start=True, stop=True), inc=False)
            vt = b.op("v", lambda e, j=j, ji=ji, xw=xw: e.tensor_scalar(
                qb[:, 0:xw], ring.bufs[j][:, 0:xw], C["q_nthr"][:, ji:ji + 1], None, ALU.is_le))
            ring.used(j, vt)
            b.wait("v", vt)
            vt = b.op("v", lambda e, xw=xw: e.tensor_tensor(qa[:, 0:xw], qa[:, 0:xw], qb[:, 0:xw], ALU.subtract))
            b.wait("g", vt)
            mk_gate = b.dma("g", dmy, bflat[kk, :, :], qa[:, 0:xw])
        b.wait("g", mk_gate)
        wq_flat = wq_d.ap().opt().rearrange("(r x) -> r x", r=N * 128)
        ag_toks.append(b.op("g", lambda e, wb_d=wb_d, wq_flat=wq_flat: e.collective_compute(
            "AllGather", ALU.bypass, replica_groups=RG8,
            ins=[wb_d.ap()], outs=[wq_flat])))
    return ag_toks[-1]


def emit_quantize(b, C, qs, wt_d, wq_d):
    nrows, ncols = wt_d.shape
    rch, cch = nrows // 128, (ncols + QW - 1) // QW
    pb, banks = C["pb"], C["banks"]
    ring = qs["ring"]
    asums = qs["asums"]
    qa = qs["qa"]
    qb = qs["qb"]
    wsrc = wt_d.ap().rearrange("(r p) c -> r p c", p=128)
    wdst = wq_d.ap().rearrange("(r p) c -> r p c", p=128)
    nt = 0
    red_toks = []
    for r in range(rch):
        for cc in range(cch):
            w = min(QW, ncols - cc * QW)
            j = ring.load("y", wsrc[r, :, cc * QW:cc * QW + w], dstview=lambda bb, w=w: bb[:, 0:w])
            b.wait("v", ring.loaded[j])
            vt = b.op("v", lambda e, j=j, nt=nt, w=w: e.tensor_reduce(
                asums[:, nt:nt + 1], ring.bufs[j][:, 0:w], AX, ALU.add, apply_absolute_value=True))
            ring.used(j, vt)
            red_toks.append(vt)
            nt += 1
    b.wait("v", red_toks[-1])
    vt = b.op("v", lambda e, nt=nt: e.tensor_reduce(C["q_tot"], asums[:, 0:nt], AX, ALU.add))
    bk = banks.acquire()
    b.wait("t", vt)
    mt = b.op("t", lambda e, bk=bk: e.matmul(pb[bk][0:1, 0:1], C["ones128_f32"][:, :], C["q_tot"], start=True, stop=True))
    b.wait("v", mt)
    ntot = float(nrows) * ncols
    vt = b.op("v", lambda e, bk=bk: e.tensor_scalar(
        C["q_thr1"], pb[bk][0:1, 0:1], 0.5 / ntot, 0.5 * QEPS, ALU.mult, ALU.add))
    banks.release(bk, vt)
    bk = banks.acquire()
    b.wait("t", vt)
    mt = b.op("t", lambda e, bk=bk: e.matmul(pb[bk][:, 0:1], C["ones1x128_f32"][:, :], C["q_thr1"], start=True, stop=True))
    b.wait("v", mt)
    vt = b.op("v", lambda e, bk=bk: e.tensor_copy(C["q_thrc"], pb[bk][:, 0:1]))
    banks.release(bk, vt)
    b.wait("v", vt)
    thr_tok = b.op("v", lambda e: e.tensor_scalar(C["q_nthr"], C["q_thrc"], -1.0, None, ALU.mult))
    mk_gate = qs["mk_gate"]
    last_store = None
    std = qs["std"]
    for r in range(rch):
        for cc in range(cch):
            w = min(QW, ncols - cc * QW)
            j = ring.load("y", wsrc[r, :, cc * QW:cc * QW + w], dstview=lambda bb, w=w: bb[:, 0:w])
            b.wait("v", ring.loaded[j])
            b.wait("v", thr_tok)
            b.wait("v", mk_gate)
            b.op("v", lambda e, j=j, w=w: e.tensor_scalar(
                qa[:, 0:w], ring.bufs[j][:, 0:w], C["q_thrc"], None, ALU.is_ge))
            vt = b.op("v", lambda e, j=j, w=w: e.tensor_scalar(
                qb[:, 0:w], ring.bufs[j][:, 0:w], C["q_nthr"], None, ALU.is_le))
            ring.used(j, vt)
            b.wait("v", vt)
            vt = b.op("v", lambda e, w=w: e.tensor_tensor(qa[:, 0:w], qa[:, 0:w], qb[:, 0:w], ALU.subtract))
            b.wait("g", vt)
            last_store = b.dma("g", std, wdst[r, :, cc * QW:cc * QW + w], qa[:, 0:w])
            mk_gate = last_store
    qs["mk_gate"] = mk_gate
    return last_store


def emit_ffn0(b, C, tag):
    xr = C["x_res"]
    abf = C["abf"]
    abf.mark()
    xn = abf.alloc(DCH * TOK)
    hid = abf.alloc(HCH * H)
    r1 = Ring(b, [abf.alloc(DCH * 128) for _ in range(2)])
    r2 = Ring(b, [abf.alloc(DCH * 128) for _ in range(2)])
    r3 = Ring(b, [abf.alloc(HCH * 128) for _ in range(2)])
    pb, banks = C["pb"], C["banks"]
    ln_tok = emit_layernorm(b, C, xr, xn, C["xres_tok"])
    wq1, wq2, wq3 = C["wq1_d"], C["wq2_d"], C["wq3_d"]
    w1v = wq1.ap().rearrange("(dc p) h -> p dc h", p=128)
    w2v = wq2.ap().rearrange("(dc p) h -> p dc h", p=128)
    w3v = wq3.ap().rearrange("(hc p) f -> p hc f", p=128)
    sig = [abf.alloc(H) for _ in range(2)]
    b.wait("y", C["wq_tok"])
    down_tok = None
    for hf in (0, 1):
        sig_gate = [None, None]
        for hc in range(HCH):
            dv = lambda bb: bb.rearrange("p (dc f) -> p dc f", dc=DCH)
            j1 = r1.load("y", w1v[:, :, hc * 128:(hc + 1) * 128], dstview=dv)
            j2 = r2.load("y", w2v[:, :, hc * 128:(hc + 1) * 128], dstview=dv)
            bA = banks.acquire()
            bB = banks.acquire()
            b.wait("t", r1.loaded[j1])
            b.wait("t", r2.loaded[j2])
            b.wait("t", ln_tok)
            v1 = r1.bufs[j1].rearrange("p (dc f) -> p dc f", dc=DCH)
            v2 = r2.bufs[j2].rearrange("p (dc f) -> p dc f", dc=DCH)
            for dc in range(DCH):
                b.op("t", lambda e, v1=v1, dc=dc, hf=hf, bA=bA: e.matmul(
                    pb[bA][:, :], v1[:, dc, :], cpart(xn, dc, hf * H, H),
                    start=(dc == 0), stop=(dc == DCH - 1)), inc=(dc == DCH - 1))
            t1 = b.tok("t")
            for dc in range(DCH):
                b.op("t", lambda e, v2=v2, dc=dc, hf=hf, bB=bB: e.matmul(
                    pb[bB][:, :], v2[:, dc, :], cpart(xn, dc, hf * H, H),
                    start=(dc == 0), stop=(dc == DCH - 1)), inc=(dc == DCH - 1))
            t2 = b.tok("t")
            r1.used(j1, t1)
            r2.used(j2, t2)
            b.wait("s", t1)
            b.wait("s", sig_gate[hc % 2])
            stok = b.op("s", lambda e, hc=hc, bA=bA: e.activation(sig[hc % 2], pb[bA][:, :], AF.Sigmoid))
            b.wait("v", stok)
            b.wait("v", t2)
            b.wait("v", down_tok)
            vtok = b.op("v", lambda e, hc=hc, bB=bB: e.tensor_tensor(
                hid[:, hc * H:(hc + 1) * H], pb[bB][:, :], sig[hc % 2], ALU.mult))
            sig_gate[hc % 2] = vtok
            banks.release(bA, vtok)
            banks.release(bB, vtok)
        hid_tok = b.tok("v")
        for fc in range(DCH):
            j3 = r3.load("y", w3v[:, :, fc * 128:(fc + 1) * 128],
                         dstview=lambda bb: bb.rearrange("p (hc f) -> p hc f", hc=HCH))
            bA = banks.acquire()
            b.wait("t", r3.loaded[j3])
            b.wait("t", hid_tok)
            v3 = r3.bufs[j3].rearrange("p (hc f) -> p hc f", hc=HCH)
            for hc in range(HCH):
                b.op("t", lambda e, v3=v3, hc=hc, bA=bA: e.matmul(
                    pb[bA][:, :], v3[:, hc, :], hid[:, hc * H:(hc + 1) * H],
                    start=(hc == 0), stop=(hc == HCH - 1)), inc=(hc == HCH - 1))
            t3 = b.tok("t")
            r3.used(j3, t3)
            b.wait("v", t3)
            vtok = b.op("v", lambda e, fc=fc, hf=hf, bA=bA: e.scalar_tensor_tensor(
                cpart(xr, fc, hf * H, H), pb[bA][:, :], 0.5, cpart(xr, fc, hf * H, H),
                ALU.mult, ALU.add))
            banks.release(bA, vtok)
        down_tok = b.tok("v")
    abf.release()
    C["xres_tok"] = b.tok("v")
    return C["xres_tok"]




def emit_rmsnorm_inplace(b, C, o, o_tok, nw):
    pb, banks = C["pb"], C["banks"]
    abf, af32 = C["abf"], C["af32"]
    abf.mark(); af32.mark()
    sq = [abf.alloc(H) for _ in range(2)]
    tsc = abf.alloc(H)
    rows = af32.alloc(2 * H)
    ms, rr = rows[0:1, 0:H], rows[0:1, H:2 * H]
    out_tok = o_tok
    for hf in (0, 1):
        bQ = banks.acquire()
        sq_gate = [None, None]
        for c in range(DCH):
            b.wait("s", sq_gate[c % 2])
            b.wait("s", out_tok)
            st = b.op("s", lambda e, c=c, hf=hf: e.activation(
                sq[c % 2], cpart(o, c, hf * H, H), AF.Square))
            b.wait("t", st)
            mt = b.op("t", lambda e, c=c, bQ=bQ: e.matmul(
                pb[bQ][0:1, :], C["ones128_bf"][:, :], sq[c % 2],
                start=(c == 0), stop=(c == DCH - 1)))
            sq_gate[c % 2] = mt
        b.wait("v", b.tok("t"))
        vt = b.op("v", lambda e, bQ=bQ: e.tensor_scalar(ms, pb[bQ][0:1, :], 1.0 / D, RMS_EPS, ALU.mult, ALU.add))
        b.wait("s", vt)
        st = b.op("s", lambda e: e.activation(ms, ms, AF.Ln))
        b.wait("s", st)
        st = b.op("s", lambda e: e.activation(rr, ms, AF.Exp, scale=-0.5))
        banks.release(bQ, st)
        bR = banks.acquire()
        b.wait("t", st)
        bt = b.op("t", lambda e, bR=bR: e.matmul(pb[bR][:, :], C["ones1x128_f32"][:, :], rr, start=True, stop=True))
        b.wait("v", bt)
        for c in range(DCH):
            b.op("v", lambda e, c=c, hf=hf, bR=bR: e.tensor_tensor(
                tsc, cpart(o, c, hf * H, H), pb[bR][:, :], ALU.mult))
            b.wait("v", b.tok("v"))
            out_tok = b.op("v", lambda e, c=c, hf=hf: e.tensor_scalar(
                cpart(o, c, hf * H, H), tsc, nw[:, c:c + 1], None, ALU.mult))
            b.wait("v", out_tok)
        banks.release(bR, out_tok)
    abf.release(); af32.release()
    return out_tok


def emit_proj_accum(b, C, wslab_dram_view, ring, xsrc, x_tok, nmm_k, out_cb, extra_rhs_tok=None):
    pb, banks = C["pb"], C["banks"]
    j = ring.load("g", wslab_dram_view,
                  dstview=lambda bb: bb.rearrange("p (dc f) -> p dc f", dc=nmm_k))
    for hf in (0, 1):
        bA = banks.acquire()
        b.wait("t", ring.loaded[j])
        b.wait("t", x_tok)
        if extra_rhs_tok is not None:
            b.wait("t", extra_rhs_tok)
        wv = ring.bufs[j].rearrange("p (dc f) -> p dc f", dc=nmm_k)
        for dc in range(nmm_k):
            b.op("t", lambda e, wv=wv, dc=dc, hf=hf, bA=bA: e.matmul(
                pb[bA][:, :], wv[:, dc, :], cpart(xsrc, dc, hf * H, H),
                start=(dc == 0), stop=(dc == nmm_k - 1)), inc=(dc == nmm_k - 1))
        mm_tok = b.tok("t")
        ring.used(j, mm_tok)
        ev_tok = out_cb(hf, bA, mm_tok)
        banks.release(bA, ev_tok)
    return b.tok("v")


def emit_gla(b, C):
    xr = C["x_res"]
    pb, banks = C["pb"], C["banks"]
    abf, af32 = C["abf"], C["af32"]
    abf.mark(); af32.mark()
    xn = abf.alloc(DCH * TOK)
    ln_tok = emit_layernorm(b, C, xr, xn, C["xres_tok"])
    b.soft_fence()
    obuf = abf.alloc(DCH * TOK)

    qdp = abf.alloc(16 * 6 * 64)
    Ubuf = abf.alloc(16 * 128)
    Sin = abf.alloc(16 * 128)
    wring = Ring(b, [abf.alloc(DCH * 128) for _ in range(4)], name="glaw")
    if GLA_HEADS < 16:
        mo = b.op("g", lambda e: e.memset(obuf, 0.0))
        mo = b.op("g", lambda e: e.memset(Ubuf, 0.0))
        mo = b.op("g", lambda e: e.memset(qdp, 0.0))
        for e_ in ("t", "v", "s", "g", "y"):
            b.wait(e_, mo)
    NP = 2
    abf.mark()
    qhx = [abf.alloc(TOK) for _ in range(NP)]
    fhx = [abf.alloc(TOK) for _ in range(NP)]
    vhx = [abf.alloc(TOK) for _ in range(NP)]
    khx = [abf.alloc(TOK) for _ in range(NP)]
    sTx = [abf.alloc(64) for _ in range(NP)]
    vcTx = [abf.alloc(128) for _ in range(NP)]
    krTx = [abf.alloc(128) for _ in range(NP)]
    krmx = [abf.alloc(64) for _ in range(NP)]
    qdcx = [abf.alloc(64) for _ in range(NP)]
    kdcx = [abf.alloc(64) for _ in range(NP)]
    Stx = [[abf.alloc(128) for _ in range(2)] for _ in range(NP)]
    zeros64 = C["zeros64"]
    af32.mark()
    Pextx = [af32.alloc(1 + TOK) for _ in range(NP)]
    nlf_sh = af32.alloc(TOK)
    nl1_sh = af32.alloc(TOK)
    nlfx = [nlf_sh] * NP
    nl1x = [nl1_sh] * NP
    eqdx = [af32.alloc(64) for _ in range(NP)]
    ekdx = [af32.alloc(64) for _ in range(NP)]
    ekrx = [af32.alloc(64) for _ in range(NP)]
    eqpx = [af32.alloc(64) for _ in range(NP)]
    colx = [af32.alloc(8) for _ in range(NP)]
    wqv = C["gla_wq"].ap().rearrange("(dc p) f -> p dc f", p=128)
    wkv = C["gla_wk"].ap().rearrange("(dc p) f -> p dc f", p=128)
    wvv = C["gla_wv"].ap().rearrange("(dc p) f -> p dc f", p=128)
    wov = C["gla_wo"].ap().rearrange("(dc p) f -> p dc f", p=128)

    HS = {}

    def prep_head(h, par):
        hs = slice(h * 128, (h + 1) * 128)
        qh, fh, vh, kh = qhx[par], fhx[par], vhx[par], khx[par]
        nlf, nl1, Pext = nlfx[par], nl1x[par], Pextx[par]

        def ev_q(hf, bA, mm_tok):
            b.wait("s", mm_tok)
            return b.op("s", lambda e, hf=hf, bA=bA: e.activation(
                qh[:, hf * H:(hf + 1) * H], pb[bA][:, :], AF.Silu))

        def ev_f(hf, bA, mm_tok):
            b.wait("s", mm_tok)
            return b.op("s", lambda e, hf=hf, bA=bA: e.activation(
                fh[:, hf * H:(hf + 1) * H], pb[bA][:, :], AF.Sigmoid))

        def ev_v(hf, bA, mm_tok):
            b.wait("v", mm_tok)
            return b.op("v", lambda e, hf=hf, bA=bA: e.tensor_copy(
                vh[:, hf * H:(hf + 1) * H], pb[bA][:, :]))
        b.wait("s", C.get(f"gla_prev_{par}"))
        b.wait("v", C.get(f"gla_prev_{par}"))
        b.wait("t", C.get(f"gla_prevt_{par}"))
        b.wait("s", C.get("gla_scan_tok"))
        emit_proj_accum(b, C, wqv[:, :, hs], wring, xn, ln_tok, DCH, ev_q)
        emit_proj_accum(b, C, wkv[:, :, hs], wring, xn, ln_tok, DCH, ev_f)
        emit_proj_accum(b, C, wvv[:, :, hs], wring, xn, ln_tok, DCH, ev_v)
        proj_s = b.tok("s")
        proj_v = b.tok("v")
        b.wait("s", proj_s)
        st = b.op("s", lambda e: e.activation(nl1, fh[:, :], AF.Exp, scale=-1.0))
        b.wait("s", st)
        st = b.op("s", lambda e: e.activation(nlf, nl1, AF.Ln, bias=1.0))
        b.wait("s", st)
        st2 = b.op("s", lambda e: e.activation(nl1, nlf, AF.Exp, scale=-1.0))
        b.wait("v", st2)
        k_tok = b.op("v", lambda e: e.tensor_scalar(kh[:, :], nl1, -1.0, 1.0, ALU.mult, ALU.add))
        vm = b.op("v", lambda e: e.memset(Pext[:, 0:1], 0.0))
        scan_toks = []
        b.wait("v", st)
        for c in range(GLA_CH):
            b.wait("v", b.tok("v"))
            scan_toks.append(b.op("v", lambda e, c=c: e.tensor_tensor_scan(
                Pext[:, 1 + c * 64:1 + (c + 1) * 64], nlf[:, c * 64:(c + 1) * 64],
                zeros64, Pext[:, c * 64:c * 64 + 1], ALU.add, ALU.add)))
        C["gla_scan_tok"] = b.tok("v")
        HS[par] = dict(h=h, k_tok=k_tok, proj_v=proj_v, scan_toks=scan_toks,
                       st_upd_tok=None, prev_t=None, e2_prev=None, e3_prev=None)

    def chunk_op(par, c):
        S = HS[par]
        h = S["h"]
        qh, kh, vh = qhx[par], khx[par], vhx[par]
        Pext = Pextx[par]
        sT, vcT, krT = sTx[par], vcTx[par], krTx[par]
        krm, qdc, kdc = krmx[par], qdcx[par], kdcx[par]
        eqd, ekd, ekr, eqp = eqdx[par], ekdx[par], ekrx[par], eqpx[par]
        noff, nPl, dec = colx[par][:, 0:1], colx[par][:, 1:2], colx[par][:, 2:3]
        St = Stx[par]
        Pc = Pext[:, 1 + c * 64:1 + (c + 1) * 64]
        off = Pext[:, c * 64:c * 64 + 1]
        Pl = Pext[:, c * 64 + 64:c * 64 + 65]
        sc_tok = S["scan_toks"][c]
        b.wait("v", sc_tok)
        b.wait("v", S["prev_t"])
        b.wait("v", S["e2_prev"])
        b.wait("v", S["e3_prev"])
        nf_tok = b.op("v", lambda e, noff=noff, off=off: e.tensor_scalar(noff, off, -1.0, None, ALU.mult))
        nl_tok = b.op("v", lambda e, nPl=nPl, Pl=Pl: e.tensor_scalar(nPl, Pl, -1.0, None, ALU.mult))
        b.wait("s", sc_tok)
        b.wait("s", S["st_upd_tok"])
        e1 = b.op("s", lambda e, eqd=eqd, Pc=Pc, off=off: e.activation(eqd, Pc, AF.Exp, scale=-1.0, bias=off))
        b.wait("s", nf_tok)
        e2 = b.op("s", lambda e, ekd=ekd, Pc=Pc, noff=noff: e.activation(ekd, Pc, AF.Exp, bias=noff))
        S["e2_prev"] = e2
        b.wait("s", nl_tok)
        e3 = b.op("s", lambda e, ekr=ekr, Pc=Pc, nPl=nPl: e.activation(ekr, Pc, AF.Exp, bias=nPl))
        S["e3_prev"] = e3
        e4 = b.op("s", lambda e, dec=dec, Pl=Pl, off=off: e.activation(dec, Pl, AF.Exp, scale=-1.0, bias=off))
        if c < 6:
            e5 = b.op("s", lambda e, eqp=eqp, Pc=Pc: e.activation(eqp, Pc, AF.Exp, scale=-1.0))
        b.wait("v", e1)
        b.wait("v", S["k_tok"])
        v1 = b.op("v", lambda e, qdc=qdc, c=c: e.tensor_tensor(
            qdc, qh[:, c * 64:(c + 1) * 64], eqd, ALU.mult))
        b.wait("v", e2)
        v2 = b.op("v", lambda e, kdc=kdc, c=c: e.tensor_tensor(
            kdc, kh[:, c * 64:(c + 1) * 64], ekd, ALU.mult))
        b.wait("v", e3)
        v3 = b.op("v", lambda e, krm=krm, c=c: e.tensor_tensor(
            krm, kh[:, c * 64:(c + 1) * 64], ekr, ALU.mult))
        if c < 6:
            b.wait("v", e5)
            b.op("v", lambda e, h=h, c=c, eqp=eqp: e.tensor_tensor(
                qdp[:, (h * 6 + c) * 64:(h * 6 + c + 1) * 64],
                qh[:, c * 64:(c + 1) * 64], eqp, ALU.mult))
        bS = banks.acquire()
        b.wait("t", v1); b.wait("t", v2)
        mt = b.op("t", lambda e, bS=bS, kdc=kdc, qdc=qdc: e.matmul(pb[bS][0:64, 0:64], kdc, qdc, start=True, stop=True))
        b.wait("v", mt)
        m_tok = b.op("v", lambda e, bS=bS, sT=sT: e.tensor_tensor(sT[0:64, :], pb[bS][0:64, 0:64], C["maskT_bf"][:, :], ALU.mult))
        banks.release(bS, m_tok)
        b.wait("t", S["proj_v"])
        b.wait("t", C.get("ptA_free"))
        tt1 = b.op("t", lambda e, c=c, vh=vh: e.transpose(C["ptA"][0:64, 0:128], vh[:, c * 64:(c + 1) * 64], C["idn_bf"][:, :]))
        b.wait("v", tt1)
        vcT_tok = b.op("v", lambda e, vcT=vcT: e.tensor_copy(vcT[0:64, :], C["ptA"][0:64, 0:128]))
        C["ptA_free"] = vcT_tok
        b.wait("t", v3)
        b.wait("t", C.get("ptB_free"))
        tt2 = b.op("t", lambda e, krm=krm: e.transpose(C["ptB"][0:64, 0:128], krm[:, 0:64], C["idn_bf"][:, :]))
        b.wait("v", tt2)
        krT_tok = b.op("v", lambda e, krT=krT: e.tensor_copy(krT[0:64, :], C["ptB"][0:64, 0:128]))
        C["ptB_free"] = krT_tok
        bO = banks.acquire()
        b.wait("t", m_tok); b.wait("t", vcT_tok)
        b.op("t", lambda e, bO=bO, c=c, vcT=vcT, sT=sT: e.matmul(pb[bO][:, 0:64], vcT[0:64, :], sT[0:64, :],
                                                                start=True, stop=(c == 0)), inc=(c == 0))
        if c > 0:
            b.wait("t", S["st_upd_tok"])
            b.op("t", lambda e, bO=bO, cur=St[c % 2], qdc=qdc: e.matmul(
                pb[bO][:, 0:64], cur, qdc, start=False, stop=True))
        mo_tok = b.tok("t")
        b.wait("v", mo_tok)
        o_evac_tok = b.op("v", lambda e, h=h, c=c, bO=bO: e.tensor_copy(
            cpart(obuf, h, c * 64, 64), pb[bO][:, 0:64]))
        banks.release(bO, o_evac_tok)
        bU = banks.acquire()
        b.wait("t", krT_tok)
        mu_tok = b.op("t", lambda e, bU=bU, krT=krT, vcT=vcT: e.matmul(pb[bU][:, 0:128], krT[0:64, :], vcT[0:64, :], start=True, stop=True))
        b.wait("v", mu_tok)
        b.wait("v", e4)
        if c == 0:
            S["st_upd_tok"] = b.op("v", lambda e, bU=bU, St=St: e.tensor_copy(St[1], pb[bU][:, 0:128]))
        else:
            S["st_upd_tok"] = b.op("v", lambda e, bU=bU, cur=St[c % 2], nxt=St[(c + 1) % 2], dec=dec: e.scalar_tensor_tensor(
                nxt, cur, dec, pb[bU][:, 0:128], ALU.mult, ALU.add))
        banks.release(bU, S["st_upd_tok"])
        S["prev_t"] = b.tok("t")

    def finish_head(par):
        S = HS[par]
        b.wait("v", S["st_upd_tok"])
        uh_tok = b.op("v", lambda e, h=S["h"], last=Stx[par][GLA_CH % 2]: e.tensor_copy(
            Ubuf[:, h * 128:(h + 1) * 128], last))
        C[f"gla_prev_{par}"] = uh_tok
        C[f"gla_prevt_{par}"] = S["prev_t"]

    npar = 2 if GLA_HEADS % 2 == 0 else 1
    for hp in range(GLA_HEADS // npar):
        pars = list(range(npar))
        for par in pars:
            prep_head(hp * npar + par, par)
        for c in range(GLA_CH):
            for par in pars:
                chunk_op(par, c)
        for par in pars:
            finish_head(par)
    C["gla_prev_head_tok"] = C.get("gla_prev_0")
    abf.release()
    if DBG_TAP and "gla_int" in DBG_TAP:
        dd = b.dsem("dbg2")
        b.wait("g", C["gla_prev_head_tok"])
        for nm, ap, shp, dt in [("dbg_qh", qh, [128, TOK], BF16), ("dbg_fh", fh, [128, TOK], BF16),
                                ("dbg_vh", vh, [128, TOK], BF16), ("dbg_kh", kh, [128, TOK], BF16),
                                ("dbg_P", Pext, [128, TOK + 1], F32), ("dbg_S", St[GLA_CH % 2], [128, 128], BF16),
                                ("dbg_xn", xn, [128, DCH * TOK], BF16)]:
            dram = b.dram(nm, shp, dt, kind="ExternalOutput")
            b.dma("g", dd, dram[:, :], ap)
    dU = b.dsem("glaU")
    b.wait("g", C.get("gla_prev_0"))
    b.wait("g", C.get("gla_prev_1"))
    ub_tok = b.dma("g", dU, C["glaU_d"][:, :], Ubuf[:, :])
    b.wait("g", ub_tok)
    cc_tok = b.op("g", lambda e: e.collective_compute(
        "AllGather", ALU.bypass, replica_groups=RG,
        ins=[C["glaU_d"].ap()], outs=[C["glaUg_d"].ap()]))
    b.wait("g", cc_tok)
    ms_tok = b.op("g", lambda e: e.memset(Sin[:, :], 0.0))
    b.wait("g", ms_tok)
    if len(RG[0]) > 1:
        dU.cnt += 16
        sin_tok = (dU.h, dU.cnt)
        ugv = C["glaUg_d"].ap().rearrange("(r p) f -> r p f", p=128)

        def dyn_load(e, ugv=ugv, Sin=Sin, sem=dU.h, v=dU.cnt):
            rank = e.cc_rank(RG)
            e.dma_start(out=Sin[:, :], in_=ugv[bass.ds(rank - 1, 1), :, :],
                        bounds_check="skip_entire_dma").then_inc(sem, 16)
        b.raw("g", dyn_load)
    else:
        sin_tok = ms_tok
    for h in range(GLA_HEADS):
        for c in range(min(6, GLA_CH)):
            bC = banks.acquire()
            b.wait("t", sin_tok)
            mt = b.op("t", lambda e, h=h, c=c, bC=bC: e.matmul(
                pb[bC][:, 0:64], Sin[:, h * 128:(h + 1) * 128],
                qdp[:, (h * 6 + c) * 64:(h * 6 + c + 1) * 64], start=True, stop=True))
            b.wait("v", mt)
            vt = b.op("v", lambda e, h=h, c=c, bC=bC: e.tensor_tensor(
                cpart(obuf, h, c * 64, 64), cpart(obuf, h, c * 64, 64),
                pb[bC][:, 0:64], ALU.add))
            banks.release(bC, vt)
    o_tok = b.tok("v")
    if DBG_TAP and "gla_o" in DBG_TAP:
        dbg = b.dram("dbg_o", [GLA_HEADS * 128, TOK], BF16, kind="ExternalOutput")
        dd = b.dsem("dbg")
        b.wait("g", o_tok)
        b.dma("g", dd, dbg.ap().rearrange("(c p) t -> p c t", p=128),
              obuf[:, 0:GLA_HEADS * TOK].rearrange("p (c t) -> p c t", c=GLA_HEADS))
    af32.release()
    b.soft_fence()
    o_tok = emit_rmsnorm_inplace(b, C, obuf, o_tok, C["gla_nw"])
    for fo in range(DCH):
        def ev_out(hf, bA, mm_tok, fo=fo):
            b.wait("v", mm_tok)
            return b.op("v", lambda e, hf=hf, bA=bA, fo=fo: e.scalar_tensor_tensor(
                cpart(xr, fo, hf * H, H), pb[bA][:, :], 1.0, cpart(xr, fo, hf * H, H),
                ALU.mult, ALU.add))
        emit_proj_accum(b, C, wov[:, :, fo * 128:(fo + 1) * 128], wring, obuf, o_tok, DCH, ev_out)
    abf.release(); af32.release()
    C["xres_tok"] = b.tok("v")
    return C["xres_tok"]




BIG = 1.0e30
ASCALE = 0.17677669529663687
ACONST = 16.0


def emit_moe(b, C, tag):
    xr = C["x_res"]
    pb, banks = C["pb"], C["banks"]
    abf, af32 = C["abf"], C["af32"]
    abf.mark(); af32.mark()
    xn = abf.alloc(DCH * TOK)
    ln_tok = emit_layernorm(b, C, xr, xn, C["xres_tok"])
    b.soft_fence()
    gw = abf.alloc(DCH * 64)
    QH = 256
    sh_hid = abf.alloc(DCH * QH)
    hg = [abf.alloc(4 * TOK) for _ in range(4)]
    wdb = [abf.alloc(TOK) for _ in range(2)]
    rsc = [abf.alloc(H) for _ in range(2)]
    tsc2 = [abf.alloc(H) for _ in range(2)]
    w1ring = Ring(b, [abf.alloc(DCH * 128) for _ in range(3)], name=f"m1{tag}")
    w2ring = Ring(b, [abf.alloc(4 * 4 * 128) for _ in range(2)], name=f"m2{tag}")
    shring = Ring(b, [abf.alloc(DCH * 128) for _ in range(3)], name=f"ms{tag}")
    gbrow = af32.alloc(64)
    gbb = af32.alloc(64)
    wd = af32.alloc(8 * 64)
    wdT = af32.alloc(8 * 128)
    wdrow = af32.alloc(TOK)
    sc_ = af32.alloc(64)
    sb2 = af32.alloc(64)
    mk = af32.alloc(64)
    wk64 = af32.alloc(64)
    gs = af32.alloc(8)
    wk8 = af32.alloc(8)
    eq8 = af32.alloc(8)
    cols = af32.alloc(8)
    m1c, m2c, t4c, t6c, ssc = (cols[:, i:i + 1] for i in range(5))

    dmo = b.dsem(f"moe{tag}")
    gw_tok = b.dma("g", dmo, gw.rearrange("p (dc e) -> p dc e", dc=DCH),
                   C["moe_gw"].ap().rearrange("(dc p) e -> p dc e", p=128))
    gb_tok = b.dma("g", dmo, gbrow[0:1, 0:64], C["moe_gb"][:, :])
    bk = banks.acquire()
    b.wait("t", gb_tok)
    mt = b.op("t", lambda e, bk=bk: e.matmul(pb[bk][:, 0:64], C["ones1x128_f32"][:, :], gbrow[0:1, 0:64], start=True, stop=True))
    b.wait("v", mt)
    gbb_tok = b.op("v", lambda e, bk=bk: e.tensor_copy(gbb, pb[bk][:, 0:64]))
    banks.release(bk, gbb_tok)

    def vnext():
        b.wait("v", b.tok("v"))

    rstate = {"wdT_tok": None}

    def do_routing_tile(tt_):
        wdT_tok = rstate["wdT_tok"]
        bk = banks.acquire()
        b.wait("t", ln_tok)
        b.wait("t", gw_tok)
        gwv = gw.rearrange("p (dc e) -> p dc e", dc=DCH)
        for dc in range(DCH):
            b.op("t", lambda e, tt_=tt_, dc=dc, bk=bk, gwv=gwv: e.matmul(
                pb[bk][:, 0:64], cpart(xn, dc, tt_ * 128, 128), gwv[:, dc, :],
                start=(dc == 0), stop=(dc == DCH - 1)), inc=(dc == DCH - 1))
        lg_tok = b.tok("t")
        b.wait("s", lg_tok)
        b.wait("s", wdT_tok)
        st = b.op("s", lambda e, bk=bk: e.activation(sc_, pb[bk][:, 0:64], AF.Sigmoid))
        banks.release(bk, st)
        b.wait("v", st)
        b.wait("v", gbb_tok)
        b.wait("v", wdT_tok)
        b.op("v", lambda e: e.tensor_tensor(sb2, sc_, gbb, ALU.add))
        vnext()
        for g in range(8):
            sbg = sb2[:, g * 8:(g + 1) * 8]
            b.op("v", lambda e, sbg=sbg: e.tensor_reduce(m1c, sbg, AX, ALU.max))
            vnext()
            b.op("v", lambda e, sbg=sbg: e.tensor_scalar(eq8, sbg, m1c, None, ALU.is_equal))
            vnext()
            b.op("v", lambda e, sbg=sbg: e.scalar_tensor_tensor(wk8, eq8, -BIG, sbg, ALU.mult, ALU.add))
            vnext()
            b.op("v", lambda e: e.tensor_reduce(m2c, wk8, AX, ALU.max))
            vnext()
            b.op("v", lambda e, g=g: e.tensor_tensor(gs[:, g:g + 1], m1c, m2c, ALU.add))
            vnext()
        b.op("v", lambda e: e.tensor_copy(wk8, gs))
        for i in range(3):
            vnext()
            b.op("v", lambda e: e.tensor_reduce(m1c, wk8, AX, ALU.max))
            vnext()
            b.op("v", lambda e: e.tensor_scalar(eq8, wk8, m1c, None, ALU.is_equal))
            vnext()
            b.op("v", lambda e: e.scalar_tensor_tensor(wk8, eq8, -BIG, wk8, ALU.mult, ALU.add))
        vnext()
        b.op("v", lambda e: e.tensor_reduce(t4c, wk8, AX, ALU.max))
        vnext()
        b.op("v", lambda e: e.tensor_scalar(gs, gs, t4c, None, ALU.is_ge))
        vnext()
        b.op("v", lambda e: e.tensor_scalar(wk8, gs, BIG, -BIG, ALU.mult, ALU.add))
        vnext()
        b.op("v", lambda e: e.tensor_tensor(
            mk.rearrange("p (g e) -> p g e", g=8), sb2.rearrange("p (g e) -> p g e", g=8),
            wk8.unsqueeze(2).broadcast_to([128, 8, 8]), ALU.add))
        vnext()
        b.op("v", lambda e: e.tensor_copy(wk64, mk))
        for i in range(5):
            vnext()
            b.op("v", lambda e: e.tensor_reduce(t6c, wk64, AX, ALU.max))
            vnext()
            b.op("v", lambda e: e.tensor_scalar(sb2, wk64, t6c, None, ALU.is_equal))
            vnext()
            b.op("v", lambda e: e.scalar_tensor_tensor(wk64, sb2, -BIG, wk64, ALU.mult, ALU.add))
        vnext()
        b.op("v", lambda e: e.tensor_reduce(t6c, wk64, AX, ALU.max))
        vnext()
        b.op("v", lambda e: e.tensor_scalar(mk, mk, t6c, None, ALU.is_ge))
        vnext()
        b.op("v", lambda e: e.tensor_tensor(mk, sc_, mk, ALU.mult))
        vnext()
        b.op("v", lambda e: e.tensor_reduce(ssc, mk, AX, ALU.add))
        vnext()
        b.op("v", lambda e: e.tensor_scalar(ssc, ssc, 1.0, 1e-20, ALU.mult, ALU.add))
        vnext()
        b.op("v", lambda e: e.reciprocal(ssc, ssc))
        vnext()
        b.op("v", lambda e: e.tensor_scalar(ssc, ssc, 2.5, None, ALU.mult))
        vnext()
        wd_tok = b.op("v", lambda e, tt_=tt_: e.tensor_scalar(wd[:, tt_ * 64:(tt_ + 1) * 64], mk, ssc, None, ALU.mult))
        bk = banks.acquire()
        b.wait("t", wd_tok)
        mt = b.op("t", lambda e, tt_=tt_, bk=bk: e.transpose(
            pb[bk][0:64, 0:128], wd[:, tt_ * 64:(tt_ + 1) * 64], C["idn_f32"][:, :]))
        b.wait("v", mt)
        wdT_tok = b.op("v", lambda e, tt_=tt_, bk=bk: e.tensor_copy(
            wdT[0:64, tt_ * 128:(tt_ + 1) * 128], pb[bk][0:64, 0:128]))
        banks.release(bk, wdT_tok)
        rstate["wdT_tok"] = wdT_tok

    upv = C["moe_shu"].ap().rearrange("(dc p) f -> p dc f", p=128)
    dnv = C["moe_shd"].ap().rearrange("(sc p) f -> p sc f", p=128)
    rsc_gate = [None, None]
    for qt in range(4):
        for sc2 in range(DCH):
            j = shring.load("g", upv[:, :, sc2 * 128:(sc2 + 1) * 128],
                            dstview=lambda bb: bb.rearrange("p (dc f) -> p dc f", dc=DCH))
            bk = banks.acquire()
            b.wait("t", shring.loaded[j])
            b.wait("t", ln_tok)
            wv = shring.bufs[j].rearrange("p (dc f) -> p dc f", dc=DCH)
            for dc in range(DCH):
                b.op("t", lambda e, wv=wv, dc=dc, qt=qt, bk=bk: e.matmul(
                    pb[bk][:, 0:QH], wv[:, dc, :], cpart(xn, dc, qt * QH, QH),
                    start=(dc == 0), stop=(dc == DCH - 1)), inc=(dc == DCH - 1))
            m_tok = b.tok("t")
            shring.used(j, m_tok)
            b.wait("s", m_tok)
            b.wait("s", rsc_gate[sc2 % 2])
            st = b.op("s", lambda e, sc2=sc2, bk=bk: e.activation(rsc[sc2 % 2][:, 0:QH], pb[bk][:, 0:QH], AF.Relu))
            b.wait("v", st)
            vt = b.op("v", lambda e, sc2=sc2, bk=bk: e.tensor_tensor(
                sh_hid[:, sc2 * QH:(sc2 + 1) * QH], pb[bk][:, 0:QH], rsc[sc2 % 2][:, 0:QH], ALU.mult))
            rsc_gate[sc2 % 2] = vt
            banks.release(bk, vt)
            if qt == 0 and sc2 < 8:
                do_routing_tile(sc2)
        hid_tok = b.tok("v")
        for fc in range(DCH):
            j = shring.load("g", dnv[:, :, fc * 128:(fc + 1) * 128],
                            dstview=lambda bb: bb.rearrange("p (sc f) -> p sc f", sc=DCH))
            bk = banks.acquire()
            b.wait("t", shring.loaded[j])
            b.wait("t", hid_tok)
            wv = shring.bufs[j].rearrange("p (sc f) -> p sc f", sc=DCH)
            for sc2 in range(DCH):
                b.op("t", lambda e, wv=wv, sc2=sc2, bk=bk: e.matmul(
                    pb[bk][:, 0:QH], wv[:, sc2, :], sh_hid[:, sc2 * QH:(sc2 + 1) * QH],
                    start=(sc2 == 0), stop=(sc2 == DCH - 1)), inc=(sc2 == DCH - 1))
            m_tok = b.tok("t")
            shring.used(j, m_tok)
            b.wait("v", m_tok)
            vt = b.op("v", lambda e, fc=fc, qt=qt, bk=bk: e.scalar_tensor_tensor(
                cpart(xr, fc, qt * QH, QH), pb[bk][:, 0:QH], 0.5, cpart(xr, fc, qt * QH, QH),
                ALU.mult, ALU.add))
            banks.release(bk, vt)

    w1v = C["moe_w1"].ap().rearrange("E (dc p) i -> p E dc i", p=128)
    w2v = C["moe_w2"].ap().rearrange("E (ic p) f -> p E ic f", p=128)
    down_tok = None
    wdrow_tok = None
    dwr = b.dsem(f"mwd{tag}")
    for grp in range(16):
        for ei in range(4):
            ex = grp * 4 + ei
            b.wait("y", rstate["wdT_tok"])
            b.wait("y", wdrow_tok)
            rtok = b.dma("y", dwr, wdrow[0:1, :], wdT[ex:ex + 1, :])
            wdb_tok = None
            for hf in (0, 1):
                bk = banks.acquire()
                b.wait("t", rtok)
                mt = b.op("t", lambda e, hf=hf, bk=bk: e.matmul(
                    pb[bk][:, :], C["ones1x128_f32"][:, :], wdrow[0:1, hf * H:(hf + 1) * H], start=True, stop=True))
                b.wait("v", mt)
                b.wait("v", b.tok("v"))
                wdb_tok = b.op("v", lambda e, hf=hf, bk=bk, ex=ex: e.tensor_copy(
                    wdb[ex % 2][:, hf * H:(hf + 1) * H], pb[bk][:, :]))
                banks.release(bk, wdb_tok)
            wdrow_tok = wdb_tok
            for ic in range(4):
                j = w1ring.load("g", w1v[:, ex, :, ic * 128:(ic + 1) * 128],
                                dstview=lambda bb: bb.rearrange("p (dc i) -> p dc i", dc=DCH))
                wv = w1ring.bufs[j].rearrange("p (dc i) -> p dc i", dc=DCH)
                for hf in (0, 1):
                    bk = banks.acquire()
                    b.wait("t", w1ring.loaded[j])
                    for dc in range(DCH):
                        b.op("t", lambda e, wv=wv, dc=dc, hf=hf, bk=bk: e.matmul(
                            pb[bk][:, :], wv[:, dc, :], cpart(xn, dc, hf * H, H),
                            start=(dc == 0), stop=(dc == DCH - 1)), inc=(dc == DCH - 1))
                    m_tok = b.tok("t")
                    w1ring.used(j, m_tok)
                    b.wait("s", m_tok)
                    b.wait("s", rsc_gate[hf])
                    st = b.op("s", lambda e, hf=hf, bk=bk: e.activation(rsc[hf][:, :], pb[bk][:, :], AF.Relu))
                    b.wait("v", st)
                    b.wait("v", wdb_tok)
                    b.wait("v", down_tok)
                    rsc_gate[hf] = b.op("v", lambda e, hf=hf, bk=bk: e.tensor_tensor(tsc2[hf][:, :], pb[bk][:, :], rsc[hf][:, :], ALU.mult))
                    vnext()
                    vt = b.op("v", lambda e, hf=hf, ei=ei, ic=ic, ex=ex: e.tensor_tensor(
                        hg[ei][:, ic * TOK + hf * H:ic * TOK + (hf + 1) * H],
                        tsc2[hf][:, :], wdb[ex % 2][:, hf * H:(hf + 1) * H], ALU.mult))
                    banks.release(bk, vt)
        up_tok = b.tok("v")
        for fc in range(DCH):
            j = w2ring.load("g", w2v[:, grp * 4:(grp + 1) * 4, :, fc * 128:(fc + 1) * 128],
                            dstview=lambda bb: bb.rearrange("p (E ic f) -> p E ic f", E=4, ic=4))
            wv = w2ring.bufs[j].rearrange("p (E ic f) -> p E ic f", E=4, ic=4)
            for hf in (0, 1):
                bk = banks.acquire()
                b.wait("t", w2ring.loaded[j])
                b.wait("t", up_tok)
                n = 0
                for ei in range(4):
                    for ic in range(4):
                        n += 1
                        b.op("t", lambda e, wv=wv, ei=ei, ic=ic, hf=hf, bk=bk, n=n: e.matmul(
                            pb[bk][:, :], wv[:, ei, ic, :],
                            hg[ei][:, ic * TOK + hf * H:ic * TOK + (hf + 1) * H],
                            start=(n == 1), stop=(n == 16)), inc=(n == 16))
                m_tok = b.tok("t")
                w2ring.used(j, m_tok)
                b.wait("v", m_tok)
                vt = b.op("v", lambda e, fc=fc, hf=hf, bk=bk: e.scalar_tensor_tensor(
                    cpart(xr, fc, hf * H, H), pb[bk][:, :], 0.5, cpart(xr, fc, hf * H, H),
                    ALU.mult, ALU.add))
                banks.release(bk, vt)
        down_tok = b.tok("t")
    abf.release(); af32.release()
    C["xres_tok"] = b.tok("v")
    return C["xres_tok"]


def emit_attn(b, C):
    xr = C["x_res"]
    pb, banks = C["pb"], C["banks"]
    abf, af32 = C["abf"], C["af32"]
    abf.mark(); af32.mark()
    qbuf = abf.alloc(DCH * TOK)
    wring = Ring(b, [abf.alloc(DCH * 128) for _ in range(4)], name="attw")
    abf.mark()
    xn = abf.alloc(DCH * TOK)
    ln_tok = emit_layernorm(b, C, xr, xn, C["xres_tok"])
    b.soft_fence()
    vfm = abf.alloc(DCH * TOK)
    ksc = [abf.alloc(H) for _ in range(2)]
    vsc = [abf.alloc(128) for _ in range(2)]
    dak = [b.dsem("atk0"), b.dsem("atk1")]
    dav = [b.dsem("atv0"), b.dsem("atv1")]
    wqv = C["at_wq"].ap().rearrange("(dc p) f -> p dc f", p=128)
    wkv = C["at_wk"].ap().rearrange("(dc p) f -> p dc f", p=128)
    wvv = C["at_wv"].ap().rearrange("(dc p) f -> p dc f", p=128)
    wov = C["at_wo"].ap().rearrange("(dc p) f -> p dc f", p=128)
    kbv = C["at_kb"].ap().rearrange("(c p) t -> p c t", p=128)
    vbv = C["at_vb"].ap().rearrange("(c p) f -> c p f", p=128)
    kst = [None, None]
    for fo in range(DCH):
        def ev_k(hf, bk, mm_tok, fo=fo):
            b.wait("v", mm_tok)
            b.wait("v", kst[hf])
            vt = b.op("v", lambda e, hf=hf, bk=bk, fo=fo: e.tensor_scalar(
                ksc[hf][:, :], pb[bk][:, :], C["at_bk"][:, fo:fo + 1], None, ALU.add))
            b.wait("y", vt)
            kst[hf] = b.dma("y", dak[hf], kbv[:, fo, hf * H:(hf + 1) * H], ksc[hf][:, :])
            return vt
        emit_proj_accum(b, C, wkv[:, :, fo * 128:(fo + 1) * 128], wring, xn, ln_tok, DCH, ev_k)
    for ds_ in dak:
        b.wait("g", (ds_.h, ds_.cnt))
    kg_tok = b.op("g", lambda e: e.collective_compute(
        "AllGather", ALU.bypass, replica_groups=RG,
        ins=[C["at_kb"].ap()], outs=[C["at_kg"].ap()]))
    vdma_gate = [None, None]
    for fo in range(DCH):
        def ev_q(hf, bk, mm_tok, fo=fo):
            b.wait("v", mm_tok)
            return b.op("v", lambda e, hf=hf, bk=bk, fo=fo: e.tensor_scalar(
                cpart(qbuf, fo, hf * H, H), pb[bk][:, :], C["at_bq"][:, fo:fo + 1], None, ALU.add))
        emit_proj_accum(b, C, wqv[:, :, fo * 128:(fo + 1) * 128], wring, xn, ln_tok, DCH, ev_q)
        vtk = [None]
        def ev_v(hf, bk, mm_tok, fo=fo, vtk=vtk):
            b.wait("v", mm_tok)
            vtk[0] = b.op("v", lambda e, hf=hf, bk=bk, fo=fo: e.tensor_scalar(
                cpart(vfm, fo, hf * H, H), pb[bk][:, :], C["at_bv"][:, fo:fo + 1], None, ALU.add))
            return vtk[0]
        emit_proj_accum(b, C, wvv[:, :, fo * 128:(fo + 1) * 128], wring, xn, ln_tok, DCH, ev_v)
        for tc in range(8):
            pt = C["ptA"] if tc % 2 == 0 else C["ptB"]
            ptfree = "ptA_free" if tc % 2 == 0 else "ptB_free"
            b.wait("t", C.get(ptfree))
            b.wait("t", vtk[0])
            mt = b.op("t", lambda e, fo=fo, tc=tc, pt=pt: e.transpose(
                pt[:, 0:128], cpart(vfm, fo, tc * 128, 128), C["idn_bf"][:, :]))
            b.wait("v", mt)
            b.wait("v", vdma_gate[tc % 2])
            vt = b.op("v", lambda e, tc=tc, pt=pt: e.tensor_copy(vsc[tc % 2][:, :], pt[:, 0:128]))
            C[ptfree] = vt
            b.wait("y", vt)
            vdma_gate[tc % 2] = b.dma("y", dav[tc % 2], vbv[tc, :, fo * 128:(fo + 1) * 128], vsc[tc % 2][:, :])
    for ds_ in dav:
        b.wait("g", (ds_.h, ds_.cnt))
    cc_tok = b.op("g", lambda e: e.collective_compute(
        "AllGather", ALU.bypass, replica_groups=RG,
        ins=[C["at_vb"].ap()], outs=[C["at_vg"].ap()]))
    abf.release()
    b.soft_fence()
    rrow = af32.alloc(2 * H)
    bcs = af32.alloc(2 * H)
    feats = abf.alloc(DCH * TOK)
    khp = [abf.alloc(len(RG[0]) * TOK) for _ in range(2)]
    vsb = [abf.alloc(130) for _ in range(3)]
    Pt = [abf.alloc(4 * H) for _ in range(2)]
    dkh = [b.dsem("akh0"), b.dsem("akh1")]
    dvh = [b.dsem("avh0"), b.dsem("avh1"), b.dsem("avh2")]
    ngrp = len(RG[0])
    SEQ = ngrp * TOK
    NKC = SEQ // 128
    kgv = C["at_kg"].ap().rearrange("(r c p) t -> r c p t", p=128, c=DCH)
    vgv = C["at_vg"].ap().rearrange("(c p) f -> c p f", p=128)
    b.wait("g", kg_tok)
    b.wait("g", cc_tok)
    on_tok = None
    for sl_ in range(3):
        on_tok = b.op("g", lambda e, sl_=sl_: e.memset(
            vsb[sl_][:, 0:130].rearrange("p (a x) -> p a x", a=2)[:, :, 64:65], 1.0))
    b.wait("t", on_tok)
    banks.take(2)
    banks.take(3)
    banks.take(4)
    obank = [pb[2], pb[3], pb[4], C["pbc"]]
    ob_free = [None, None, None, None]
    kuse = [None, None]
    khp_tok = [None, None]
    vuse = [None, None, None]
    pt_use = [None, None]
    nrm_gate = [None, None]
    for hp in range(16):
        jj = hp % 2
        b.wait("g", kuse[jj])
        for r in range(ngrp):
            khp_tok[jj] = b.dma("g", dkh[jj], khp[jj][:, r * TOK:(r + 1) * TOK], kgv[r, hp, :, :])
        av_last = [None, None, None, None]
        for tkc in range(NKC):
            vj = tkc % 3
            b.wait("g", vuse[vj])
            vs_tok = b.dma("g", dvh[vj],
                           vsb[vj][:, 0:130].rearrange("p (a x) -> p a x", a=2)[:, :, 0:64],
                           vgv[tkc, :, hp * 128:(hp + 1) * 128].rearrange("p (a x) -> p a x", a=2))
            pj = tkc % 2
            b.wait("s", pt_use[pj])
            exp_toks = []
            for h01 in (0, 1):
                base = h01 * 64
                for hf in (0, 1):
                    ch = h01 * 2 + hf
                    bk = banks.acquire()
                    b.wait("t", khp_tok[jj])
                    b.wait("t", ln_tok)
                    mt = b.op("t", lambda e, jj=jj, base=base, tkc=tkc, hf=hf, hp=hp, bk=bk: e.matmul(
                        pb[bk][:, :], khp[jj][base:base + 64, tkc * 128:(tkc + 1) * 128],
                        cpart(qbuf, hp, hf * H, H)[base:base + 64, :], start=True, stop=True))
                    b.wait("s", mt)
                    st = b.op("s", lambda e, pj=pj, ch=ch, bk=bk: e.activation(
                        Pt[pj][:, ch * H:(ch + 1) * H], pb[bk][:, :], AF.Exp, scale=ASCALE, bias=C["neg16"]))
                    banks.release(bk, st)
                    exp_toks.append((ch, st))
            for ch, st in exp_toks:
                h01 = ch // 2
                b.wait("t", st)
                b.wait("t", vs_tok)
                if tkc == 0:
                    b.wait("t", ob_free[ch])
                amt = b.op("t", lambda e, pj=pj, ch=ch, h01=h01, tkc=tkc: e.matmul(
                    obank[ch][0:65, :], vsb[tkc % 3][:, h01 * 65:(h01 + 1) * 65],
                    Pt[pj][:, ch * H:(ch + 1) * H], start=(tkc == 0), stop=(tkc == NKC - 1)))
                av_last[ch] = amt
            vuse[vj] = av_last[3]
            pt_use[pj] = av_last[3]
        kuse[jj] = b.tok("t")
        for ch in range(4):
            h01, hf = ch // 2, ch % 2
            nb = ch % 2
            b.wait("v", av_last[ch])
            b.wait("v", nrm_gate[nb])
            b.op("v", lambda e, ch=ch, nb=nb: e.reciprocal(rrow[0:1, nb * H:(nb + 1) * H], obank[ch][64:65, :]))
            rr_tok = b.tok("v")
            bk = banks.acquire()
            b.wait("t", rr_tok)
            mt = b.op("t", lambda e, bk=bk, nb=nb: e.matmul(
                pb[bk][0:64, :], C["ones1x128_f32"][:, 0:64], rrow[0:1, nb * H:(nb + 1) * H], start=True, stop=True))
            b.wait("v", mt)
            b.op("v", lambda e, bk=bk, nb=nb: e.tensor_copy(bcs[0:64, nb * H:(nb + 1) * H], pb[bk][0:64, :]))
            b.wait("v", b.tok("v"))
            ft = b.op("v", lambda e, hp=hp, hf=hf, h01=h01, ch=ch, nb=nb: e.tensor_tensor(
                cpart(feats, hp, hf * H, H)[h01 * 64:h01 * 64 + 64, :], obank[ch][0:64, :],
                bcs[0:64, nb * H:(nb + 1) * H], ALU.mult))
            banks.release(bk, ft)
            ob_free[ch] = ft
            nrm_gate[nb] = ft
    feats_tok = b.tok("v")
    banks.untake(2)
    banks.untake(3)
    banks.untake(4)
    for fo in range(DCH):
        def ev_out(hf, bk, mm_tok, fo=fo):
            b.wait("v", mm_tok)
            return b.op("v", lambda e, hf=hf, bk=bk, fo=fo: e.scalar_tensor_tensor(
                cpart(xr, fo, hf * H, H), pb[bk][:, :], C["at_bo"][:, fo:fo + 1],
                ALU.add, ALU.add, ) if False else e.scalar_tensor_tensor(
                cpart(xr, fo, hf * H, H), pb[bk][:, :], C["at_bo"][:, fo:fo + 1],
                cpart(xr, fo, hf * H, H), ALU.add, ALU.add))
        emit_proj_accum(b, C, wov[:, :, fo * 128:(fo + 1) * 128], wring, feats, feats_tok, DCH, ev_out)
    abf.release(); af32.release()
    C["xres_tok"] = b.tok("v")
    return C["xres_tok"]



CONSTS = {
    "idn_bf": ([128, 128], BF16),
    "idn_f32": ([128, 128], F32),
    "ones128_bf": ([128, 1], BF16),
    "ones128_f32": ([128, 1], F32),
    "ones1x128_f32": ([1, 128], F32),
    "maskT_bf": ([64, 64], BF16),
}


def build_program(parts):
    b = Builder()
    C = {}
    xin_d = b.dram("xin", [D, TOK], F32, kind="ExternalInput")
    xout_d = b.dram("xout", [D, TOK], F32, kind="ExternalOutput")
    need_ffn0 = any(p in parts for p in ("ffn0a", "ffn0b"))
    need_gla = "gla" in parts
    wd_in = {}
    if need_ffn0:
        for n, shape in (("w1t", [D, HID]), ("w2t", [D, HID]), ("w3t", [HID, D])):
            wd_in[n] = b.dram(n, shape, F32, kind="ExternalInput")
        qsp = "Shared" if len(RG8[0]) > 1 else "Local"
        C["wq1_d"] = b.nc.dram_tensor("wq1", [D, HID], BF16, kind="Internal", addr_space=qsp)
        C["wq2_d"] = b.nc.dram_tensor("wq2", [D, HID], BF16, kind="Internal", addr_space=qsp)
        C["wq3_d"] = b.nc.dram_tensor("wq3", [HID, D], BF16, kind="Internal", addr_space=qsp)
    if need_gla:
        for n in ("gla_wq", "gla_wk", "gla_wv", "gla_wo"):
            C[n] = b.dram(n, [D, D], F32, kind="ExternalInput")
        C["gla_nw_d"] = b.dram("gla_nw", [128, DCH], F32, kind="ExternalInput")
        C["glaU_d"] = b.dram("glaU_b", [128, 16 * 128], BF16)
        C["glaUg_d"] = b.dram("glaU_g", [len(RG[0]) * 128, 16 * 128], BF16)
    need_moe = any(p in parts for p in ("moe1a", "moe1b"))
    need_attn = "attn1" in parts
    if need_moe:
        C["moe_gw"] = b.dram("moe_gw", [D, 64], F32, kind="ExternalInput")
        C["moe_gb"] = b.dram("moe_gb", [1, 64], F32, kind="ExternalInput")
        C["moe_w1"] = b.dram("moe_w1", [64, D, 512], F32, kind="ExternalInput")
        C["moe_w2"] = b.dram("moe_w2", [64, 512, D], F32, kind="ExternalInput")
        C["moe_shu"] = b.dram("moe_shu", [D, D], F32, kind="ExternalInput")
        C["moe_shd"] = b.dram("moe_shd", [D, D], F32, kind="ExternalInput")
    if need_attn:
        for n in ("at_wq", "at_wk", "at_wv", "at_wo"):
            C[n] = b.dram(n, [D, D], F32, kind="ExternalInput")
        for n in ("at_bq_d", "at_bk_d", "at_bv_d", "at_bo_d"):
            C[n] = b.dram(n, [128, DCH], F32, kind="ExternalInput")
        ng = len(RG[0])
        C["at_kb"] = b.dram("at_kb", [D, TOK], BF16)
        C["at_vb"] = b.dram("at_vb", [TOK, D], BF16)
        C["at_kg"] = b.dram("at_kg", [ng * D, TOK], BF16)
        C["at_vg"] = b.dram("at_vg", [ng * TOK, D], BF16)

    for n, (shape, dt) in CONSTS.items():
        C[n] = b.sb(n, shape, dt)
    C["x_res"] = b.sb("x_res", [128, DCH * TOK], F32)
    C["zeros64"] = b.sb("zeros64", [128, 64], F32)[:, :]
    if need_gla:
        C["gla_nw"] = b.sb("gla_nw_sb", [128, DCH], F32)[:, :]
    if need_attn:
        for n in ("at_bq", "at_bk", "at_bv", "at_bo"):
            C[n] = b.sb(n + "_sb", [128, DCH], F32)[:, :]
        C["neg16"] = b.sb("neg16", [128, 1], F32)[:, :]
    C["abf"] = Arena(b, "abf", 60 * 1024, BF16)
    C["af32"] = Arena(b, "af32", 5120, F32)
    if need_ffn0:
        C["q_tot"] = b.sb("q_tot", [128, 1], F32)[:, :]
        C["q_thr1"] = b.sb("q_thr1", [1, 1], F32)[:, :]
        C["q_thrc"] = b.sb("q_thrc", [128, 4], F32)[:, :]
        C["q_nthr"] = b.sb("q_nthr", [128, 4], F32)[:, :]
        C["q_ptot"] = b.sb("q_ptot", [1, 8], F32)[:, :]
        C["qar_in"] = b.dram("qar_in", [1, 8], F32)
        C["qar_out"] = b.dram("qar_out", [1, 8], F32)
        NQ = len(RG8[0])
        for ji, nel in enumerate((D * HID, D * HID, HID * D)):
            slq = nel // NQ
            nchq = (slq + 128 * QW - 1) // (128 * QW)
            C[f"qb{ji}_d"] = b.dram(f"qb{ji}", [128 * nchq, slq // (128 * nchq)], BF16)

    C["pb"] = [b.ps(f"pb{i}", [128, 512], F32) for i in range(5)]
    C["ptA"] = b.ps("ptA", [128, 512], BF16)
    C["ptB"] = b.ps("ptB", [128, 512], BF16)
    C["pbc"] = b.ps("pbc", [128, 512], F32)
    C["banks"] = Banks(b, C["pb"])

    din = b.dsem("init")
    for n in CONSTS:
        cd = b.dram("c_" + n, list(CONSTS[n][0]), CONSTS[n][1], kind="ExternalInput")
        b.dma("g", din, C[n][:, :], cd[:, :])
    if need_gla:
        b.dma("g", din, C["gla_nw"], C["gla_nw_d"][:, :])
    if need_attn:
        for n in ("at_bq", "at_bk", "at_bv", "at_bo"):
            b.dma("g", din, C[n], C[n + "_d"][:, :])
    init_tok = b.dma("g", din, C["x_res"].ap().rearrange("p (c t) -> p c t", c=DCH),
                     xin_d.ap().rearrange("(c p) t -> p c t", p=128))
    zm_tok = b.op("g", lambda e: e.memset(C["zeros64"], 0.0))
    if need_attn:
        zm_tok = b.op("g", lambda e: e.memset(C["neg16"], -ACONST))
    C["xres_tok"] = None
    for e in ENG_ATTR:
        b.wait(e, init_tok)
        b.wait(e, zm_tok)

    C["wq_tok"] = None
    if need_ffn0:
        b.phase("quant")
        C["abf"].reset(); C["af32"].reset()
        abf, af32 = C["abf"], C["af32"]
        abf.mark(); af32.mark()
        qs = {
            "ring": Ring(b, [af32.alloc(QW) for _ in range(2)], name="qring"),
            "asums": af32.alloc(256),
            "qa": abf.alloc(QW),
            "qb": abf.alloc(QW),
            "ptot": abf.alloc(8).bitcast(F32)[0:1, 0:8] if False else C["q_ptot"],
            "std": b.dsem("qstore"),
        }
        C["wq_tok"] = emit_quantize_coop(b, C, qs, [
            (wd_in["w1t"], C["wq1_d"]),
            (wd_in["w2t"], C["wq2_d"]),
            (wd_in["w3t"], C["wq3_d"]),
        ])
        abf.release(); af32.release()

    for p in parts:
        b.phase(p)
        b.dma_fence()
        C["abf"].reset()
        C["af32"].reset()
        if p in ("ffn0a", "ffn0b"):
            emit_ffn0(b, C, p)
        elif p == "gla":
            emit_gla(b, C)
        elif p in ("moe1a", "moe1b"):
            emit_moe(b, C, p)
        elif p == "attn1":
            emit_attn(b, C)
        else:
            raise NotImplementedError(p)

    b.phase("fin")
    b.wait("g", C["xres_tok"])
    dout = b.dsem("out")
    otok = b.dma("g", dout, xout_d.ap().rearrange("(c p) t -> p c t", p=128),
                 C["x_res"].ap().rearrange("p (c t) -> p c t", c=DCH))
    b.raw("g", lambda e, sem=otok[0], v=otok[1]: e.wait_ge(sem, v))
    return _finalize(b), ["xout"]



def _consts_np():
    return {
        "c_idn_bf": np.eye(128, dtype=ml_dtypes.bfloat16),
        "c_idn_f32": np.eye(128, dtype=np.float32),
        "c_ones128_bf": np.ones((128, 1), ml_dtypes.bfloat16),
        "c_ones128_f32": np.ones((128, 1), np.float32),
        "c_ones1x128_f32": np.ones((1, 128), np.float32),
        "c_maskT_bf": np.triu(np.ones((64, 64))).astype(ml_dtypes.bfloat16),
    }


_PROG_CACHE = {}


def _get_prog(parts):
    key = tuple(parts)
    if key not in _PROG_CACHE:
        _PROG_CACHE[key] = build_program(list(parts))
    return _PROG_CACHE[key]


def _weights_np(parts, inp):
    w = {}
    if any(p in parts for p in ("ffn0a", "ffn0b")):
        w["w1t"] = np.ascontiguousarray(inp["L0_ffn_w1"].T)
        w["w2t"] = np.ascontiguousarray(inp["L0_ffn_w2"].T)
        w["w3t"] = np.ascontiguousarray(inp["L0_ffn_w3"].T)
    if "gla" in parts:
        w["gla_wq"] = np.ascontiguousarray(inp["L0_attn_Wq"].T)
        w["gla_wk"] = np.ascontiguousarray(inp["L0_attn_Wk"].T)
        w["gla_wv"] = np.ascontiguousarray(inp["L0_attn_Wv"].T)
        w["gla_wo"] = np.ascontiguousarray(inp["L0_attn_Wo"].T)
        w["gla_nw"] = np.ascontiguousarray(inp["L0_attn_norm_w"].reshape(DCH, 128).T)
    if any(p in parts for p in ("moe1a", "moe1b")):
        w["moe_gw"] = np.ascontiguousarray(inp["L1_ffn_gate_w"].T)
        w["moe_gb"] = np.ascontiguousarray(inp["L1_ffn_gate_bias"].reshape(1, 64))
        w["moe_w1"] = np.ascontiguousarray(inp["L1_ffn_w1"].transpose(0, 2, 1))
        w["moe_w2"] = np.ascontiguousarray(inp["L1_ffn_w2"].transpose(0, 2, 1))
        w["moe_shu"] = np.ascontiguousarray(inp["L1_ffn_shared_up"].T)
        w["moe_shd"] = np.ascontiguousarray(inp["L1_ffn_shared_down"].T)
    if "attn1" in parts:
        for a, bnm in (("wq", "Wq"), ("wk", "Wk"), ("wv", "Wv"), ("wo", "Wo")):
            w["at_" + a] = np.ascontiguousarray(inp["L1_attn_" + bnm])
        for a, bnm in (("bq", "bq"), ("bk", "bk"), ("bv", "bv"), ("bo", "bo")):
            w["at_" + a + "_d"] = np.ascontiguousarray(inp["L1_attn_" + bnm].reshape(DCH, 128).T)
    return w


def run_parts(groups, inp):
    x = inp["L0_attn_x"]
    cur = []
    for c in range(NC):
        bb, p = c // 4, c % 4
        cur.append(np.ascontiguousarray(x[bb, p * TOK:(p + 1) * TOK, :].T))
    consts = _consts_np()
    for parts in groups:
        nc, outs = _get_prog(tuple(parts))
        w = _weights_np(parts, inp)
        in_maps = [dict(xin=cur[c], **w, **consts) for c in range(NC)]
        res = run_bass_kernel_spmd(nc, in_maps, core_ids=list(range(NC)))
        cur = [np.asarray(res.results[c]["xout"]) for c in range(NC)]
    out = np.empty((2, 4096, D), np.float32)
    for c in range(NC):
        bb, p = c // 4, c % 4
        out[bb, p * TOK:(p + 1) * TOK, :] = cur[c].T
    return out


ALL_PARTS = ["ffn0a", "gla", "ffn0b", "moe1a", "attn1", "moe1b"]
GROUPS = [ALL_PARTS]


def kernel(**inp):
    return run_parts(GROUPS, inp)


# version 07 (final)
# speedup vs baseline: 1.0682x; 1.0682x over previous
import contextlib
import numpy as np
import ml_dtypes
import concourse.bass as bass
import concourse.mybir as mybir
from concourse.bass_utils import run_bass_kernel_spmd

F32 = mybir.dt.float32
BF16 = mybir.dt.bfloat16
AF = mybir.ActivationFunctionType
ALU = mybir.AluOpType
AX = mybir.AxisListType.X

NC = 8
TOK = 1024
D = 2048
DCH = D // 128
HID = 5632
HCH = HID // 128
H = 512
RG = [[0, 1, 2, 3], [4, 5, 6, 7]]
RG8 = [[0, 1, 2, 3, 4, 5, 6, 7]]
GLA_HEADS = 16
GLA_CH = 16
DBG_TAP = None
LN_EPS = 1e-5
RMS_EPS = 1e-6
QEPS = 1e-8
QW = 1408

ENG_ATTR = {"t": "tensor", "v": "vector", "s": "scalar", "g": "gpsimd", "y": "sync"}


class DSem:
    def __init__(self, handle):
        self.h = handle
        self.cnt = 0


class Builder:
    def __init__(self):
        self.nc = bass.Bass("TRN2", target_bir_lowering=False)
        self.es = contextlib.ExitStack()
        self.phases = []
        self.sem = {}
        self.cnt = {}
        self.waited = {}
        self.dsems = {}
        for e in ENG_ATTR:
            self.sem[e] = self.es.enter_context(self.nc.semaphore(f"S{e}"))
            self.cnt[e] = 0
        self.cur = None
        self.phase("init")

    def dsem(self, name):
        if name not in self.dsems:
            self.dsems[name] = DSem(self.es.enter_context(self.nc.semaphore(f"D{name}")))
        return self.dsems[name]

    def phase(self, name):
        self.cur = {e: [] for e in ENG_ATTR}
        self.cur["name"] = name
        self.phases.append(self.cur)

    def sb(self, name, shape, dt=F32):
        return self.es.enter_context(self.nc.sbuf_tensor(name, shape, dt))

    def ps(self, name, shape, dt=F32):
        return self.es.enter_context(self.nc.psum_tensor(name, shape, dt))

    def dram(self, name, shape, dt=F32, kind="Internal"):
        return self.nc.dram_tensor(name, shape, dt, kind=kind)

    def op(self, eng, maker, inc=True):
        sem = self.sem[eng] if inc else None
        if inc:
            self.cnt[eng] += 1
        tgt = self.cnt[eng]

        def fn(e, maker=maker, sem=sem):
            inst = maker(e)
            if sem is not None:
                inst.then_inc(sem, 1)
        self.cur[eng].append(fn)
        return (self.sem[eng], tgt)

    def tok(self, eng):
        return (self.sem[eng], self.cnt[eng])

    def raw(self, eng, fn):
        self.cur[eng].append(fn)

    def wait(self, eng, tok):
        if tok is None:
            return
        sem, v = tok
        if v <= 0:
            return
        key = (eng, sem)
        if self.waited.get(key, 0) >= v:
            return
        self.waited[key] = v

        def fn(e, sem=sem, v=v):
            e.wait_ge(sem, v)
        self.cur[eng].append(fn)

    def waits(self, eng, toks):
        for t in toks:
            self.wait(eng, t)

    def soft_fence(self, engines=("t", "v", "s", "g", "y")):
        for e in engines:
            for p in ("t", "v", "s"):
                self.wait(e, (self.sem[p], self.cnt[p]))

    def dma_fence(self):
        for e in ENG_ATTR:
            for ds in self.dsems.values():
                self.wait(e, (ds.h, ds.cnt))

    def dma(self, queue, ds, out, in_, **kw):
        ds.cnt += 16
        tgt = ds.cnt
        sem = ds.h

        def fn(e, out=out, in_=in_, kw=kw, sem=sem):
            e.dma_start(out=out, in_=in_, **kw).then_inc(sem, 16)
        self.cur[queue].append(fn)
        return (sem, tgt)


def _finalize(b):
    nc = b.nc
    for ph in b.phases:
        with nc.Block(ph["name"]) as blk:
            for ename, attr in ENG_ATTR.items():
                ops = ph[ename]
                if not ops:
                    continue
                getattr(blk, attr)(lambda e, ops=ops: [f(e) for f in ops])
    b.es.close()
    return nc


class Arena:
    def __init__(self, b, name, els, dt):
        self.t = b.sb(name, [128, els], dt)
        self.els = els
        self.off = 0
        self.marks = []
        self.peak = 0

    def alloc(self, els):
        a = self.off
        self.off += els
        self.peak = max(self.peak, self.off)
        assert self.off <= self.els, f"arena {self.t.name} overflow {self.off}>{self.els}"
        return self.t[:, a:a + els]

    def mark(self):
        self.marks.append(self.off)

    def release(self):
        self.off = self.marks.pop()

    def reset(self):
        self.off = 0
        self.marks = []


class Ring:
    _uid = 0

    def __init__(self, b, bufs, name=None):
        self.b = b
        self.bufs = bufs
        self.n = len(bufs)
        self.i = 0
        self.free = [[] for _ in range(self.n)]
        self.loaded = [None] * self.n
        if name is None:
            name = f"ring{Ring._uid}"
            Ring._uid += 1
        self.ds = [b.dsem(f"{name}_{j}") for j in range(self.n)]

    def load(self, queue, in_, dstview=None, **kw):
        j = self.i % self.n
        self.i += 1
        for t in self.free[j]:
            self.b.wait(queue, t)
        self.free[j] = []
        dst = self.bufs[j] if dstview is None else dstview(self.bufs[j])
        self.loaded[j] = self.b.dma(queue, self.ds[j], dst, in_, **kw)
        return j

    def used(self, j, tok):
        self.free[j].append(tok)


class Banks:
    def __init__(self, b, handles):
        self.b = b
        self.h = handles
        self.n = len(handles)
        self.i = 0
        self.free = [None] * self.n

    def acquire(self):
        for _ in range(self.n):
            j = self.i % self.n
            self.i += 1
            if j in getattr(self, "taken", ()):
                continue
            self.b.wait("t", self.free[j])
            return j
        raise RuntimeError("no free banks")

    def release(self, j, tok):
        self.free[j] = tok

    def take(self, j):
        if not hasattr(self, "taken"):
            self.taken = set()
        self.b.wait("t", self.free[j])
        self.taken.add(j)

    def untake(self, j):
        self.taken.discard(j)


def cpart(buf, c, t0, tn):
    return buf[:, c * TOK + t0: c * TOK + t0 + tn]



def emit_layernorm(b, C, src, dst, src_tok):
    pb, banks = C["pb"], C["banks"]
    abf, af32 = C["abf"], C["af32"]
    abf.mark(); af32.mark()
    sq = [abf.alloc(TOK) for _ in range(2)]
    tsc = [abf.alloc(H) for _ in range(2)]
    rows = af32.alloc(6 * H)
    mu, e2, t1, t2, rr, ss = (rows[0:1, i * H:(i + 1) * H] for i in range(6))
    b.wait("s", src_tok)
    bS = [banks.acquire(), banks.acquire()]
    bQ = [banks.acquire(), banks.acquire()]
    sq_gate = [None] * 2
    for c in range(DCH):
        ct = b.op("s", lambda e, c=c: e.copy(cpart(dst, c, 0, TOK), cpart(src, c, 0, TOK)))
        b.wait("s", ct)
        b.wait("s", sq_gate[c % 2])
        st = b.op("s", lambda e, c=c: e.activation(sq[c % 2], cpart(dst, c, 0, TOK), AF.Square))
        b.wait("t", ct)
        for hf in (0, 1):
            b.op("t", lambda e, c=c, hf=hf: e.matmul(
                pb[bS[hf]][0:1, :], C["ones128_bf"][:, :], cpart(dst, c, hf * H, H),
                start=(c == 0), stop=(c == DCH - 1)), inc=False)
        b.wait("t", st)
        for hf in (0, 1):
            mt = b.op("t", lambda e, c=c, hf=hf: e.matmul(
                pb[bQ[hf]][0:1, :], C["ones128_bf"][:, :], sq[c % 2][:, hf * H:(hf + 1) * H],
                start=(c == 0), stop=(c == DCH - 1)), inc=True)
        sq_gate[c % 2] = mt
    stat_tok = b.tok("t")
    out_tok = None
    bb = []
    for hf in (0, 1):
        b.wait("v", stat_tok)
        b.op("v", lambda e, hf=hf: e.tensor_scalar(mu, pb[bS[hf]][0:1, :], 1.0 / D, None, ALU.mult))
        b.op("v", lambda e, hf=hf: e.tensor_scalar(e2, pb[bQ[hf]][0:1, :], 1.0 / D, None, ALU.mult))
        b.wait("v", b.tok("v"))
        b.op("v", lambda e: e.tensor_tensor(t1, mu, mu, ALU.mult))
        b.wait("v", b.tok("v"))
        b.op("v", lambda e: e.scalar_tensor_tensor(t2, e2, 1.0, t1, ALU.mult, ALU.subtract))
        b.wait("v", b.tok("v"))
        vtok = b.op("v", lambda e: e.tensor_scalar(t2, t2, LN_EPS, None, ALU.add))
        b.wait("s", vtok)
        st = b.op("s", lambda e: e.activation(t1, t2, AF.Ln))
        b.wait("t", vtok)
        b.op("t", lambda e: e.matmul(C["pbc"][0:1, 0:1], C["ones128_bf"][:, :],
                                     C["ones128_bf"][:, :], start=True, stop=True), inc=False)
        b.wait("s", st)
        st = b.op("s", lambda e: e.activation(rr, t1, AF.Exp, scale=-0.5))
        b.wait("v", st)
        vtok = b.op("v", lambda e: e.tensor_tensor(ss, mu, rr, ALU.mult))
        banks.release(bS[hf], vtok)
        bR = banks.acquire()
        bT = bQ[hf]
        b.wait("t", vtok)
        b.op("t", lambda e, bR=bR: e.matmul(pb[bR][:, :], C["ones1x128_f32"][:, :], rr, start=True, stop=True))
        btok = b.op("t", lambda e, bT=bT: e.matmul(pb[bT][0:128, :], C["ones1x128_f32"][:, :], ss, start=True, stop=True))
        b.wait("v", btok)
        tgate = [None, None]
        for c in range(DCH):
            b.wait("v", tgate[c % 2])
            mtok = b.op("v", lambda e, c=c, hf=hf, bR=bR: e.tensor_tensor(
                tsc[c % 2], cpart(dst, c, hf * H, H), pb[bR][:, :], ALU.mult))
            b.wait("v", mtok)
            out_tok = b.op("v", lambda e, c=c, hf=hf, bT=bT: e.tensor_tensor(
                cpart(dst, c, hf * H, H), tsc[c % 2], pb[bT][0:128, :], ALU.subtract))
            tgate[c % 2] = out_tok
            b.wait("t", mtok)
            b.op("t", lambda e: e.matmul(C["pbc"][0:1, 0:1], C["ones128_bf"][:, :],
                                         C["ones128_bf"][:, :], start=True, stop=True), inc=False)
        banks.release(bR, out_tok)
        bb.append((bT, out_tok))
    for bT, tk in bb:
        banks.release(bT, tk)
    abf.release(); af32.release()
    return out_tok


def emit_quantize_coop(b, C, qs, jobs):
    pb, banks = C["pb"], C["banks"]
    N = len(RG8[0])
    rankcell = {}

    def _rank(e):
        if id(e) not in rankcell:
            rankcell[id(e)] = e.cc_rank(RG8)
        return rankcell[id(e)]
    ring = qs["ring"]
    asums = qs["asums"]
    qa, qb = qs["qa"], qs["qb"]
    dmy = qs["std"]
    nj = len(jobs)
    ptot = qs["ptot"]
    b.op("v", lambda e: e.memset(ptot, 0.0))
    red_tok = None
    ntg = 0
    for ji, (wt_d, wq_d) in enumerate(jobs):
        nel = int(wt_d.shape[0]) * int(wt_d.shape[1])
        sl = nel // N
        nch = (sl + 128 * QW - 1) // (128 * QW)
        wflat = wt_d.ap().opt().rearrange("(r k p x) -> r k p x", r=N, k=nch, p=128, x=sl // (128 * nch))
        xw = sl // (128 * nch)
        nt = 0
        for kk in range(nch):
            def ld(e, wflat=wflat, kk=kk, dst=None):
                pass
            j = ring.i % ring.n
            for t_ in ring.free[j]:
                b.wait("g", t_)
            ring.free[j] = []
            ring.i += 1
            dsem = ring.ds[j]
            dsem.cnt += 16
            tokv = (dsem.h, dsem.cnt)
            wk = wflat[:, kk, :, :]
            def fn(e, wk=wk, dst=ring.bufs[j][:, 0:xw], sem=dsem.h):
                rank = _rank(e)
                e.dma_start(out=dst, in_=wk[bass.ds(rank, 1), :, :]).then_inc(sem, 16)
            b.cur["g"].append(fn)
            ring.loaded[j] = tokv
            b.wait("v", tokv)
            vt = b.op("v", lambda e, j=j, c0=ntg + nt, xw=xw: e.tensor_reduce(
                asums[:, c0:c0 + 1], ring.bufs[j][:, 0:xw], AX, ALU.add,
                apply_absolute_value=True))
            ring.used(j, vt)
            b.wait("t", vt)
            for _ in range(2):
                b.op("t", lambda e: e.matmul(C["pbc"][0:1, 0:1], C["ones128_bf"][:, :],
                                             C["ones128_bf"][:, :], start=True, stop=True), inc=False)
            nt += 1
        b.wait("v", b.tok("v"))
        vt = b.op("v", lambda e, c0=ntg, nt=nt: e.tensor_reduce(
            C["q_tot"], asums[:, c0:c0 + nt], AX, ALU.add))
        ntg += nt
        bk = banks.acquire()
        b.wait("t", vt)
        mt = b.op("t", lambda e, bk=bk, ji=ji: e.matmul(pb[bk][0:1, ji:ji + 1], C["ones128_f32"][:, :], C["q_tot"], start=True, stop=True))
        b.wait("v", mt)
        red_tok = b.op("v", lambda e, bk=bk, ji=ji: e.tensor_copy(ptot[0:1, ji:ji + 1], pb[bk][0:1, ji:ji + 1]))
        banks.release(bk, red_tok)
    b.wait("g", red_tok)
    ar_in, ar_out = C["qar_in"], C["qar_out"]
    dq = b.dsem("qcoop")
    t_ = b.dma("g", dq, ar_in[:, :], ptot[0:1, 0:8])
    b.wait("g", t_)
    cc = b.op("g", lambda e: e.collective_compute(
        "AllReduce", ALU.add, replica_groups=RG8, ins=[ar_in.ap()], outs=[ar_out.ap()]))
    b.wait("g", cc)
    t_ = b.dma("g", dq, ptot[0:1, 0:8], ar_out[:, :])
    b.wait("v", t_)
    for ji, (wt_d, wq_d) in enumerate(jobs):
        nel = float(int(wt_d.shape[0]) * int(wt_d.shape[1]))
        vt = b.op("v", lambda e, ji=ji, nel=nel: e.tensor_scalar(
            C["q_thr1"], ptot[0:1, ji:ji + 1], 0.5 / nel, 0.5 * QEPS, ALU.mult, ALU.add))
        bk = banks.acquire()
        b.wait("t", vt)
        mt = b.op("t", lambda e, bk=bk: e.matmul(pb[bk][:, 0:1], C["ones1x128_f32"][:, :], C["q_thr1"], start=True, stop=True))
        b.wait("v", mt)
        b.wait("v", b.tok("v"))
        vt = b.op("v", lambda e, bk=bk, ji=ji: e.tensor_copy(C["q_thrc"][:, ji:ji + 1], pb[bk][:, 0:1]))
        banks.release(bk, vt)
        b.wait("v", vt)
        thr_tok = b.op("v", lambda e, ji=ji: e.tensor_scalar(
            C["q_nthr"][:, ji:ji + 1], C["q_thrc"][:, ji:ji + 1], -1.0, None, ALU.mult))
    mk_gate = None
    ag_toks = []
    for ji, (wt_d, wq_d) in enumerate(jobs):
        nel = int(wt_d.shape[0]) * int(wt_d.shape[1])
        sl = nel // N
        nch = (sl + 128 * QW - 1) // (128 * QW)
        xw = sl // (128 * nch)
        wflat = wt_d.ap().opt().rearrange("(r k p x) -> r k p x", r=N, k=nch, p=128, x=xw)
        wb_d = C[f"qb{ji}_d"]
        bflat = wb_d.ap().rearrange("(k p) x -> k p x", p=128)
        for kk in range(nch):
            j = ring.i % ring.n
            for t2 in ring.free[j]:
                b.wait("g", t2)
            ring.free[j] = []
            ring.i += 1
            dsem = ring.ds[j]
            dsem.cnt += 16
            tokv = (dsem.h, dsem.cnt)
            wk = wflat[:, kk, :, :]
            def fn(e, wk=wk, dst=ring.bufs[j][:, 0:xw], sem=dsem.h):
                rank = _rank(e)
                e.dma_start(out=dst, in_=wk[bass.ds(rank, 1), :, :]).then_inc(sem, 16)
            b.cur["g"].append(fn)
            ring.loaded[j] = tokv
            b.wait("v", tokv)
            b.wait("v", thr_tok)
            b.wait("v", mk_gate)
            b.op("v", lambda e, j=j, ji=ji, xw=xw: e.tensor_scalar(
                qa[:, 0:xw], ring.bufs[j][:, 0:xw], C["q_thrc"][:, ji:ji + 1], None, ALU.is_ge))
            b.wait("t", b.tok("v"))
            for _ in range(2):
                b.op("t", lambda e: e.matmul(C["pbc"][0:1, 0:1], C["ones128_bf"][:, :],
                                             C["ones128_bf"][:, :], start=True, stop=True), inc=False)
            vt = b.op("v", lambda e, j=j, ji=ji, xw=xw: e.tensor_scalar(
                qb[:, 0:xw], ring.bufs[j][:, 0:xw], C["q_nthr"][:, ji:ji + 1], None, ALU.is_le))
            ring.used(j, vt)
            b.wait("v", vt)
            vt = b.op("v", lambda e, xw=xw: e.tensor_tensor(qa[:, 0:xw], qa[:, 0:xw], qb[:, 0:xw], ALU.subtract))
            b.wait("g", vt)
            mk_gate = b.dma("g", dmy, bflat[kk, :, :], qa[:, 0:xw])
        b.wait("g", mk_gate)
        wq_flat = wq_d.ap().opt().rearrange("(r x) -> r x", r=N * 128)
        ag_toks.append(b.op("g", lambda e, wb_d=wb_d, wq_flat=wq_flat: e.collective_compute(
            "AllGather", ALU.bypass, replica_groups=RG8,
            ins=[wb_d.ap()], outs=[wq_flat])))
    return ag_toks[-1]


def emit_quantize(b, C, qs, wt_d, wq_d):
    nrows, ncols = wt_d.shape
    rch, cch = nrows // 128, (ncols + QW - 1) // QW
    pb, banks = C["pb"], C["banks"]
    ring = qs["ring"]
    asums = qs["asums"]
    qa = qs["qa"]
    qb = qs["qb"]
    wsrc = wt_d.ap().rearrange("(r p) c -> r p c", p=128)
    wdst = wq_d.ap().rearrange("(r p) c -> r p c", p=128)
    nt = 0
    red_toks = []
    for r in range(rch):
        for cc in range(cch):
            w = min(QW, ncols - cc * QW)
            j = ring.load("y", wsrc[r, :, cc * QW:cc * QW + w], dstview=lambda bb, w=w: bb[:, 0:w])
            b.wait("v", ring.loaded[j])
            vt = b.op("v", lambda e, j=j, nt=nt, w=w: e.tensor_reduce(
                asums[:, nt:nt + 1], ring.bufs[j][:, 0:w], AX, ALU.add, apply_absolute_value=True))
            ring.used(j, vt)
            red_toks.append(vt)
            nt += 1
    b.wait("v", red_toks[-1])
    vt = b.op("v", lambda e, nt=nt: e.tensor_reduce(C["q_tot"], asums[:, 0:nt], AX, ALU.add))
    bk = banks.acquire()
    b.wait("t", vt)
    mt = b.op("t", lambda e, bk=bk: e.matmul(pb[bk][0:1, 0:1], C["ones128_f32"][:, :], C["q_tot"], start=True, stop=True))
    b.wait("v", mt)
    ntot = float(nrows) * ncols
    vt = b.op("v", lambda e, bk=bk: e.tensor_scalar(
        C["q_thr1"], pb[bk][0:1, 0:1], 0.5 / ntot, 0.5 * QEPS, ALU.mult, ALU.add))
    banks.release(bk, vt)
    bk = banks.acquire()
    b.wait("t", vt)
    mt = b.op("t", lambda e, bk=bk: e.matmul(pb[bk][:, 0:1], C["ones1x128_f32"][:, :], C["q_thr1"], start=True, stop=True))
    b.wait("v", mt)
    vt = b.op("v", lambda e, bk=bk: e.tensor_copy(C["q_thrc"], pb[bk][:, 0:1]))
    banks.release(bk, vt)
    b.wait("v", vt)
    thr_tok = b.op("v", lambda e: e.tensor_scalar(C["q_nthr"], C["q_thrc"], -1.0, None, ALU.mult))
    mk_gate = qs["mk_gate"]
    last_store = None
    std = qs["std"]
    for r in range(rch):
        for cc in range(cch):
            w = min(QW, ncols - cc * QW)
            j = ring.load("y", wsrc[r, :, cc * QW:cc * QW + w], dstview=lambda bb, w=w: bb[:, 0:w])
            b.wait("v", ring.loaded[j])
            b.wait("v", thr_tok)
            b.wait("v", mk_gate)
            b.op("v", lambda e, j=j, w=w: e.tensor_scalar(
                qa[:, 0:w], ring.bufs[j][:, 0:w], C["q_thrc"], None, ALU.is_ge))
            vt = b.op("v", lambda e, j=j, w=w: e.tensor_scalar(
                qb[:, 0:w], ring.bufs[j][:, 0:w], C["q_nthr"], None, ALU.is_le))
            ring.used(j, vt)
            b.wait("v", vt)
            vt = b.op("v", lambda e, w=w: e.tensor_tensor(qa[:, 0:w], qa[:, 0:w], qb[:, 0:w], ALU.subtract))
            b.wait("g", vt)
            last_store = b.dma("g", std, wdst[r, :, cc * QW:cc * QW + w], qa[:, 0:w])
            mk_gate = last_store
    qs["mk_gate"] = mk_gate
    return last_store


def emit_ffn0(b, C, tag):
    xr = C["x_res"]
    abf = C["abf"]
    abf.mark()
    xn = abf.alloc(DCH * TOK)
    hid = abf.alloc(HCH * H)
    r1 = Ring(b, [abf.alloc(DCH * 128) for _ in range(2)])
    r2 = Ring(b, [abf.alloc(DCH * 128) for _ in range(2)])
    r3 = Ring(b, [abf.alloc(HCH * 128) for _ in range(2)])
    pb, banks = C["pb"], C["banks"]
    ln_tok = emit_layernorm(b, C, xr, xn, C["xres_tok"])
    wq1, wq2, wq3 = C["wq1_d"], C["wq2_d"], C["wq3_d"]
    w1v = wq1.ap().rearrange("(dc p) h -> p dc h", p=128)
    w2v = wq2.ap().rearrange("(dc p) h -> p dc h", p=128)
    w3v = wq3.ap().rearrange("(hc p) f -> p hc f", p=128)
    sig = [abf.alloc(H) for _ in range(2)]
    b.wait("y", C["wq_tok"])
    down_tok = None
    for hf in (0, 1):
        sig_gate = [None, None]
        for hc in range(HCH):
            dv = lambda bb: bb.rearrange("p (dc f) -> p dc f", dc=DCH)
            j1 = r1.load("y", w1v[:, :, hc * 128:(hc + 1) * 128], dstview=dv)
            j2 = r2.load("y", w2v[:, :, hc * 128:(hc + 1) * 128], dstview=dv)
            bA = banks.acquire()
            bB = banks.acquire()
            b.wait("t", r1.loaded[j1])
            b.wait("t", r2.loaded[j2])
            b.wait("t", ln_tok)
            v1 = r1.bufs[j1].rearrange("p (dc f) -> p dc f", dc=DCH)
            v2 = r2.bufs[j2].rearrange("p (dc f) -> p dc f", dc=DCH)
            for dc in range(DCH):
                b.op("t", lambda e, v1=v1, dc=dc, hf=hf, bA=bA: e.matmul(
                    pb[bA][:, :], v1[:, dc, :], cpart(xn, dc, hf * H, H),
                    start=(dc == 0), stop=(dc == DCH - 1)), inc=(dc == DCH - 1))
            t1 = b.tok("t")
            for dc in range(DCH):
                b.op("t", lambda e, v2=v2, dc=dc, hf=hf, bB=bB: e.matmul(
                    pb[bB][:, :], v2[:, dc, :], cpart(xn, dc, hf * H, H),
                    start=(dc == 0), stop=(dc == DCH - 1)), inc=(dc == DCH - 1))
            t2 = b.tok("t")
            r1.used(j1, t1)
            r2.used(j2, t2)
            b.wait("s", t1)
            b.wait("s", sig_gate[hc % 2])
            stok = b.op("s", lambda e, hc=hc, bA=bA: e.activation(sig[hc % 2], pb[bA][:, :], AF.Sigmoid))
            b.wait("v", stok)
            b.wait("v", t2)
            b.wait("v", down_tok)
            vtok = b.op("v", lambda e, hc=hc, bB=bB: e.tensor_tensor(
                hid[:, hc * H:(hc + 1) * H], pb[bB][:, :], sig[hc % 2], ALU.mult))
            sig_gate[hc % 2] = vtok
            banks.release(bA, vtok)
            banks.release(bB, vtok)
        hid_tok = b.tok("v")
        for fc in range(DCH):
            j3 = r3.load("y", w3v[:, :, fc * 128:(fc + 1) * 128],
                         dstview=lambda bb: bb.rearrange("p (hc f) -> p hc f", hc=HCH))
            bA = banks.acquire()
            b.wait("t", r3.loaded[j3])
            b.wait("t", hid_tok)
            v3 = r3.bufs[j3].rearrange("p (hc f) -> p hc f", hc=HCH)
            for hc in range(HCH):
                b.op("t", lambda e, v3=v3, hc=hc, bA=bA: e.matmul(
                    pb[bA][:, :], v3[:, hc, :], hid[:, hc * H:(hc + 1) * H],
                    start=(hc == 0), stop=(hc == HCH - 1)), inc=(hc == HCH - 1))
            t3 = b.tok("t")
            r3.used(j3, t3)
            b.wait("v", t3)
            vtok = b.op("v", lambda e, fc=fc, hf=hf, bA=bA: e.scalar_tensor_tensor(
                cpart(xr, fc, hf * H, H), pb[bA][:, :], 0.5, cpart(xr, fc, hf * H, H),
                ALU.mult, ALU.add))
            banks.release(bA, vtok)
        down_tok = b.tok("v")
    abf.release()
    C["xres_tok"] = b.tok("v")
    return C["xres_tok"]




def emit_rmsnorm_inplace(b, C, o, o_tok, nw):
    pb, banks = C["pb"], C["banks"]
    abf, af32 = C["abf"], C["af32"]
    abf.mark(); af32.mark()
    sq = [abf.alloc(H) for _ in range(2)]
    tsc = abf.alloc(H)
    rows = af32.alloc(2 * H)
    ms, rr = rows[0:1, 0:H], rows[0:1, H:2 * H]
    out_tok = o_tok
    for hf in (0, 1):
        bQ = banks.acquire()
        sq_gate = [None, None]
        for c in range(DCH):
            b.wait("s", sq_gate[c % 2])
            b.wait("s", out_tok)
            st = b.op("s", lambda e, c=c, hf=hf: e.activation(
                sq[c % 2], cpart(o, c, hf * H, H), AF.Square))
            b.wait("t", st)
            mt = b.op("t", lambda e, c=c, bQ=bQ: e.matmul(
                pb[bQ][0:1, :], C["ones128_bf"][:, :], sq[c % 2],
                start=(c == 0), stop=(c == DCH - 1)))
            sq_gate[c % 2] = mt
        b.wait("v", b.tok("t"))
        vt = b.op("v", lambda e, bQ=bQ: e.tensor_scalar(ms, pb[bQ][0:1, :], 1.0 / D, RMS_EPS, ALU.mult, ALU.add))
        b.wait("s", vt)
        st = b.op("s", lambda e: e.activation(ms, ms, AF.Ln))
        b.wait("s", st)
        st = b.op("s", lambda e: e.activation(rr, ms, AF.Exp, scale=-0.5))
        banks.release(bQ, st)
        bR = banks.acquire()
        b.wait("t", st)
        bt = b.op("t", lambda e, bR=bR: e.matmul(pb[bR][:, :], C["ones1x128_f32"][:, :], rr, start=True, stop=True))
        b.wait("v", bt)
        for c in range(DCH):
            b.op("v", lambda e, c=c, hf=hf, bR=bR: e.tensor_tensor(
                tsc, cpart(o, c, hf * H, H), pb[bR][:, :], ALU.mult))
            b.wait("v", b.tok("v"))
            out_tok = b.op("v", lambda e, c=c, hf=hf: e.tensor_scalar(
                cpart(o, c, hf * H, H), tsc, nw[:, c:c + 1], None, ALU.mult))
            b.wait("v", out_tok)
        banks.release(bR, out_tok)
    abf.release(); af32.release()
    return out_tok


def emit_proj_accum(b, C, wslab_dram_view, ring, xsrc, x_tok, nmm_k, out_cb, extra_rhs_tok=None):
    pb, banks = C["pb"], C["banks"]
    j = ring.load("g", wslab_dram_view,
                  dstview=lambda bb: bb.rearrange("p (dc f) -> p dc f", dc=nmm_k))
    for hf in (0, 1):
        bA = banks.acquire()
        b.wait("t", ring.loaded[j])
        b.wait("t", x_tok)
        if extra_rhs_tok is not None:
            b.wait("t", extra_rhs_tok)
        wv = ring.bufs[j].rearrange("p (dc f) -> p dc f", dc=nmm_k)
        for dc in range(nmm_k):
            b.op("t", lambda e, wv=wv, dc=dc, hf=hf, bA=bA: e.matmul(
                pb[bA][:, :], wv[:, dc, :], cpart(xsrc, dc, hf * H, H),
                start=(dc == 0), stop=(dc == nmm_k - 1)), inc=(dc == nmm_k - 1))
        mm_tok = b.tok("t")
        ring.used(j, mm_tok)
        ev_tok = out_cb(hf, bA, mm_tok)
        banks.release(bA, ev_tok)
    return b.tok("v")


def emit_gla(b, C):
    xr = C["x_res"]
    pb, banks = C["pb"], C["banks"]
    abf, af32 = C["abf"], C["af32"]
    abf.mark(); af32.mark()
    xn = abf.alloc(DCH * TOK)
    ln_tok = emit_layernorm(b, C, xr, xn, C["xres_tok"])
    b.soft_fence()
    obuf = abf.alloc(DCH * TOK)

    qdp = abf.alloc(16 * 6 * 64)
    Ubuf = abf.alloc(16 * 128)
    Sin = abf.alloc(16 * 128)
    wring = Ring(b, [abf.alloc(DCH * 128) for _ in range(4)], name="glaw")
    if GLA_HEADS < 16:
        mo = b.op("g", lambda e: e.memset(obuf, 0.0))
        mo = b.op("g", lambda e: e.memset(Ubuf, 0.0))
        mo = b.op("g", lambda e: e.memset(qdp, 0.0))
        for e_ in ("t", "v", "s", "g", "y"):
            b.wait(e_, mo)
    NP = 2
    abf.mark()
    qhx = [abf.alloc(TOK) for _ in range(NP)]
    fhx = [abf.alloc(TOK) for _ in range(NP)]
    vhx = [abf.alloc(TOK) for _ in range(NP)]
    khx = [abf.alloc(TOK) for _ in range(NP)]
    sTx = [abf.alloc(64) for _ in range(NP)]
    vcTx = [abf.alloc(128) for _ in range(NP)]
    krTx = [abf.alloc(128) for _ in range(NP)]
    krmx = [abf.alloc(64) for _ in range(NP)]
    qdcx = [abf.alloc(64) for _ in range(NP)]
    kdcx = [abf.alloc(64) for _ in range(NP)]
    Stx = [[abf.alloc(128) for _ in range(2)] for _ in range(NP)]
    zeros64 = C["zeros64"]
    af32.mark()
    Pextx = [af32.alloc(1 + TOK) for _ in range(NP)]
    nlf_sh = af32.alloc(TOK)
    nl1_sh = af32.alloc(TOK)
    nlfx = [nlf_sh] * NP
    nl1x = [nl1_sh] * NP
    eqdx = [af32.alloc(64) for _ in range(NP)]
    ekdx = [af32.alloc(64) for _ in range(NP)]
    ekrx = [af32.alloc(64) for _ in range(NP)]
    eqpx = [af32.alloc(64) for _ in range(NP)]
    colx = [af32.alloc(8) for _ in range(NP)]
    wqv = C["gla_wq"].ap().rearrange("(dc p) f -> p dc f", p=128)
    wkv = C["gla_wk"].ap().rearrange("(dc p) f -> p dc f", p=128)
    wvv = C["gla_wv"].ap().rearrange("(dc p) f -> p dc f", p=128)
    wov = C["gla_wo"].ap().rearrange("(dc p) f -> p dc f", p=128)

    HS = {}

    def prep_head(h, par):
        hs = slice(h * 128, (h + 1) * 128)
        qh, fh, vh, kh = qhx[par], fhx[par], vhx[par], khx[par]
        nlf, nl1, Pext = nlfx[par], nl1x[par], Pextx[par]

        def ev_q(hf, bA, mm_tok):
            b.wait("s", mm_tok)
            return b.op("s", lambda e, hf=hf, bA=bA: e.activation(
                qh[:, hf * H:(hf + 1) * H], pb[bA][:, :], AF.Silu))

        def ev_f(hf, bA, mm_tok):
            b.wait("s", mm_tok)
            return b.op("s", lambda e, hf=hf, bA=bA: e.activation(
                fh[:, hf * H:(hf + 1) * H], pb[bA][:, :], AF.Sigmoid))

        def ev_v(hf, bA, mm_tok):
            b.wait("v", mm_tok)
            return b.op("v", lambda e, hf=hf, bA=bA: e.tensor_copy(
                vh[:, hf * H:(hf + 1) * H], pb[bA][:, :]))
        b.wait("s", C.get(f"gla_prev_{par}"))
        b.wait("v", C.get(f"gla_prev_{par}"))
        b.wait("t", C.get(f"gla_prevt_{par}"))
        b.wait("s", C.get("gla_scan_tok"))
        emit_proj_accum(b, C, wqv[:, :, hs], wring, xn, ln_tok, DCH, ev_q)
        emit_proj_accum(b, C, wkv[:, :, hs], wring, xn, ln_tok, DCH, ev_f)
        emit_proj_accum(b, C, wvv[:, :, hs], wring, xn, ln_tok, DCH, ev_v)
        proj_s = b.tok("s")
        proj_v = b.tok("v")
        b.wait("s", proj_s)
        st = b.op("s", lambda e: e.activation(nl1, fh[:, :], AF.Exp, scale=-1.0))
        b.wait("s", st)
        st = b.op("s", lambda e: e.activation(nlf, nl1, AF.Ln, bias=1.0))
        b.wait("s", st)
        st2 = b.op("s", lambda e: e.activation(nl1, nlf, AF.Exp, scale=-1.0))
        b.wait("v", st2)
        k_tok = b.op("v", lambda e: e.tensor_scalar(kh[:, :], nl1, -1.0, 1.0, ALU.mult, ALU.add))
        vm = b.op("v", lambda e: e.memset(Pext[:, 0:1], 0.0))
        scan_toks = []
        b.wait("v", st)
        for c in range(GLA_CH):
            b.wait("v", b.tok("v"))
            scan_toks.append(b.op("v", lambda e, c=c: e.tensor_tensor_scan(
                Pext[:, 1 + c * 64:1 + (c + 1) * 64], nlf[:, c * 64:(c + 1) * 64],
                zeros64, Pext[:, c * 64:c * 64 + 1], ALU.add, ALU.add)))
        C["gla_scan_tok"] = b.tok("v")
        HS[par] = dict(h=h, k_tok=k_tok, proj_v=proj_v, scan_toks=scan_toks,
                       st_upd_tok=None, prev_t=None, e2_prev=None, e3_prev=None)

    def chunk_op(par, c):
        S = HS[par]
        h = S["h"]
        qh, kh, vh = qhx[par], khx[par], vhx[par]
        Pext = Pextx[par]
        sT, vcT, krT = sTx[par], vcTx[par], krTx[par]
        krm, qdc, kdc = krmx[par], qdcx[par], kdcx[par]
        eqd, ekd, ekr, eqp = eqdx[par], ekdx[par], ekrx[par], eqpx[par]
        noff, nPl, dec = colx[par][:, 0:1], colx[par][:, 1:2], colx[par][:, 2:3]
        St = Stx[par]
        Pc = Pext[:, 1 + c * 64:1 + (c + 1) * 64]
        off = Pext[:, c * 64:c * 64 + 1]
        Pl = Pext[:, c * 64 + 64:c * 64 + 65]
        sc_tok = S["scan_toks"][c]
        b.wait("v", sc_tok)
        b.wait("v", S["prev_t"])
        b.wait("v", S["e2_prev"])
        b.wait("v", S["e3_prev"])
        nf_tok = b.op("v", lambda e, noff=noff, off=off: e.tensor_scalar(noff, off, -1.0, None, ALU.mult))
        nl_tok = b.op("v", lambda e, nPl=nPl, Pl=Pl: e.tensor_scalar(nPl, Pl, -1.0, None, ALU.mult))
        b.wait("s", sc_tok)
        b.wait("s", S["st_upd_tok"])
        e1 = b.op("s", lambda e, eqd=eqd, Pc=Pc, off=off: e.activation(eqd, Pc, AF.Exp, scale=-1.0, bias=off))
        b.wait("s", nf_tok)
        e2 = b.op("s", lambda e, ekd=ekd, Pc=Pc, noff=noff: e.activation(ekd, Pc, AF.Exp, bias=noff))
        S["e2_prev"] = e2
        b.wait("s", nl_tok)
        e3 = b.op("s", lambda e, ekr=ekr, Pc=Pc, nPl=nPl: e.activation(ekr, Pc, AF.Exp, bias=nPl))
        S["e3_prev"] = e3
        e4 = b.op("s", lambda e, dec=dec, Pl=Pl, off=off: e.activation(dec, Pl, AF.Exp, scale=-1.0, bias=off))
        if c < 6:
            e5 = b.op("s", lambda e, eqp=eqp, Pc=Pc: e.activation(eqp, Pc, AF.Exp, scale=-1.0))
        b.wait("v", e1)
        b.wait("v", S["k_tok"])
        v1 = b.op("v", lambda e, qdc=qdc, c=c: e.tensor_tensor(
            qdc, qh[:, c * 64:(c + 1) * 64], eqd, ALU.mult))
        b.wait("v", e2)
        v2 = b.op("v", lambda e, kdc=kdc, c=c: e.tensor_tensor(
            kdc, kh[:, c * 64:(c + 1) * 64], ekd, ALU.mult))
        b.wait("v", e3)
        v3 = b.op("v", lambda e, krm=krm, c=c: e.tensor_tensor(
            krm, kh[:, c * 64:(c + 1) * 64], ekr, ALU.mult))
        if c < 6:
            b.wait("v", e5)
            b.op("v", lambda e, h=h, c=c, eqp=eqp: e.tensor_tensor(
                qdp[:, (h * 6 + c) * 64:(h * 6 + c + 1) * 64],
                qh[:, c * 64:(c + 1) * 64], eqp, ALU.mult))
        bS = banks.acquire()
        b.wait("t", v1); b.wait("t", v2)
        mt = b.op("t", lambda e, bS=bS, kdc=kdc, qdc=qdc: e.matmul(pb[bS][0:64, 0:64], kdc, qdc, start=True, stop=True))
        b.wait("v", mt)
        m_tok = b.op("v", lambda e, bS=bS, sT=sT: e.tensor_tensor(sT[0:64, :], pb[bS][0:64, 0:64], C["maskT_bf"][:, :], ALU.mult))
        banks.release(bS, m_tok)
        b.wait("t", S["proj_v"])
        b.wait("t", C.get("ptA_free"))
        tt1 = b.op("t", lambda e, c=c, vh=vh: e.transpose(C["ptA"][0:64, 0:128], vh[:, c * 64:(c + 1) * 64], C["idn_bf"][:, :]))
        b.wait("v", tt1)
        vcT_tok = b.op("v", lambda e, vcT=vcT: e.tensor_copy(vcT[0:64, :], C["ptA"][0:64, 0:128]))
        C["ptA_free"] = vcT_tok
        b.wait("t", v3)
        b.wait("t", C.get("ptB_free"))
        tt2 = b.op("t", lambda e, krm=krm: e.transpose(C["ptB"][0:64, 0:128], krm[:, 0:64], C["idn_bf"][:, :]))
        b.wait("v", tt2)
        krT_tok = b.op("v", lambda e, krT=krT: e.tensor_copy(krT[0:64, :], C["ptB"][0:64, 0:128]))
        C["ptB_free"] = krT_tok
        bO = banks.acquire()
        b.wait("t", m_tok); b.wait("t", vcT_tok)
        b.op("t", lambda e, bO=bO, c=c, vcT=vcT, sT=sT: e.matmul(pb[bO][:, 0:64], vcT[0:64, :], sT[0:64, :],
                                                                start=True, stop=(c == 0)), inc=(c == 0))
        if c > 0:
            b.wait("t", S["st_upd_tok"])
            b.op("t", lambda e, bO=bO, cur=St[c % 2], qdc=qdc: e.matmul(
                pb[bO][:, 0:64], cur, qdc, start=False, stop=True))
        mo_tok = b.tok("t")
        b.wait("v", mo_tok)
        o_evac_tok = b.op("v", lambda e, h=h, c=c, bO=bO: e.tensor_copy(
            cpart(obuf, h, c * 64, 64), pb[bO][:, 0:64]))
        banks.release(bO, o_evac_tok)
        bU = banks.acquire()
        b.wait("t", krT_tok)
        mu_tok = b.op("t", lambda e, bU=bU, krT=krT, vcT=vcT: e.matmul(pb[bU][:, 0:128], krT[0:64, :], vcT[0:64, :], start=True, stop=True))
        b.wait("v", mu_tok)
        b.wait("v", e4)
        if c == 0:
            S["st_upd_tok"] = b.op("v", lambda e, bU=bU, St=St: e.tensor_copy(St[1], pb[bU][:, 0:128]))
        else:
            S["st_upd_tok"] = b.op("v", lambda e, bU=bU, cur=St[c % 2], nxt=St[(c + 1) % 2], dec=dec: e.scalar_tensor_tensor(
                nxt, cur, dec, pb[bU][:, 0:128], ALU.mult, ALU.add))
        banks.release(bU, S["st_upd_tok"])
        S["prev_t"] = b.tok("t")

    def finish_head(par):
        S = HS[par]
        b.wait("v", S["st_upd_tok"])
        uh_tok = b.op("v", lambda e, h=S["h"], last=Stx[par][GLA_CH % 2]: e.tensor_copy(
            Ubuf[:, h * 128:(h + 1) * 128], last))
        C[f"gla_prev_{par}"] = uh_tok
        C[f"gla_prevt_{par}"] = S["prev_t"]

    npar = 2 if GLA_HEADS % 2 == 0 else 1
    for hp in range(GLA_HEADS // npar):
        pars = list(range(npar))
        for par in pars:
            prep_head(hp * npar + par, par)
        for c in range(GLA_CH):
            for par in pars:
                chunk_op(par, c)
        for par in pars:
            finish_head(par)
    C["gla_prev_head_tok"] = C.get("gla_prev_0")
    abf.release()
    if DBG_TAP and "gla_int" in DBG_TAP:
        dd = b.dsem("dbg2")
        b.wait("g", C["gla_prev_head_tok"])
        for nm, ap, shp, dt in [("dbg_qh", qh, [128, TOK], BF16), ("dbg_fh", fh, [128, TOK], BF16),
                                ("dbg_vh", vh, [128, TOK], BF16), ("dbg_kh", kh, [128, TOK], BF16),
                                ("dbg_P", Pext, [128, TOK + 1], F32), ("dbg_S", St[GLA_CH % 2], [128, 128], BF16),
                                ("dbg_xn", xn, [128, DCH * TOK], BF16)]:
            dram = b.dram(nm, shp, dt, kind="ExternalOutput")
            b.dma("g", dd, dram[:, :], ap)
    dU = b.dsem("glaU")
    b.wait("g", C.get("gla_prev_0"))
    b.wait("g", C.get("gla_prev_1"))
    ub_tok = b.dma("g", dU, C["glaU_d"][:, :], Ubuf[:, :])
    b.wait("g", ub_tok)
    cc_tok = b.op("g", lambda e: e.collective_compute(
        "AllGather", ALU.bypass, replica_groups=RG,
        ins=[C["glaU_d"].ap()], outs=[C["glaUg_d"].ap()]))
    b.wait("g", cc_tok)
    ms_tok = b.op("g", lambda e: e.memset(Sin[:, :], 0.0))
    b.wait("g", ms_tok)
    if len(RG[0]) > 1:
        dU.cnt += 16
        sin_tok = (dU.h, dU.cnt)
        ugv = C["glaUg_d"].ap().rearrange("(r p) f -> r p f", p=128)

        def dyn_load(e, ugv=ugv, Sin=Sin, sem=dU.h, v=dU.cnt):
            rank = e.cc_rank(RG)
            e.dma_start(out=Sin[:, :], in_=ugv[bass.ds(rank - 1, 1), :, :],
                        bounds_check="skip_entire_dma").then_inc(sem, 16)
        b.raw("g", dyn_load)
    else:
        sin_tok = ms_tok
    for h in range(GLA_HEADS):
        for c in range(min(6, GLA_CH)):
            bC = banks.acquire()
            b.wait("t", sin_tok)
            mt = b.op("t", lambda e, h=h, c=c, bC=bC: e.matmul(
                pb[bC][:, 0:64], Sin[:, h * 128:(h + 1) * 128],
                qdp[:, (h * 6 + c) * 64:(h * 6 + c + 1) * 64], start=True, stop=True))
            b.wait("v", mt)
            vt = b.op("v", lambda e, h=h, c=c, bC=bC: e.tensor_tensor(
                cpart(obuf, h, c * 64, 64), cpart(obuf, h, c * 64, 64),
                pb[bC][:, 0:64], ALU.add))
            banks.release(bC, vt)
    o_tok = b.tok("v")
    if DBG_TAP and "gla_o" in DBG_TAP:
        dbg = b.dram("dbg_o", [GLA_HEADS * 128, TOK], BF16, kind="ExternalOutput")
        dd = b.dsem("dbg")
        b.wait("g", o_tok)
        b.dma("g", dd, dbg.ap().rearrange("(c p) t -> p c t", p=128),
              obuf[:, 0:GLA_HEADS * TOK].rearrange("p (c t) -> p c t", c=GLA_HEADS))
    af32.release()
    b.soft_fence()
    o_tok = emit_rmsnorm_inplace(b, C, obuf, o_tok, C["gla_nw"])
    for fo in range(DCH):
        def ev_out(hf, bA, mm_tok, fo=fo):
            b.wait("v", mm_tok)
            return b.op("v", lambda e, hf=hf, bA=bA, fo=fo: e.scalar_tensor_tensor(
                cpart(xr, fo, hf * H, H), pb[bA][:, :], 1.0, cpart(xr, fo, hf * H, H),
                ALU.mult, ALU.add))
        emit_proj_accum(b, C, wov[:, :, fo * 128:(fo + 1) * 128], wring, obuf, o_tok, DCH, ev_out)
    abf.release(); af32.release()
    C["xres_tok"] = b.tok("v")
    return C["xres_tok"]




BIG = 1.0e30
ASCALE = 0.17677669529663687
ACONST = 16.0


def emit_moe(b, C, tag):
    xr = C["x_res"]
    pb, banks = C["pb"], C["banks"]
    abf, af32 = C["abf"], C["af32"]
    abf.mark(); af32.mark()
    xn = abf.alloc(DCH * TOK)
    ln_tok = emit_layernorm(b, C, xr, xn, C["xres_tok"])
    b.soft_fence()
    gw = abf.alloc(DCH * 64)
    QH = 256
    sh_hid = abf.alloc(DCH * QH)
    hg = [abf.alloc(4 * TOK) for _ in range(4)]
    wdb = [abf.alloc(TOK) for _ in range(2)]
    rsc = [abf.alloc(H) for _ in range(2)]
    tsc2 = [abf.alloc(H) for _ in range(2)]
    w1ring = Ring(b, [abf.alloc(DCH * 128) for _ in range(3)], name=f"m1{tag}")
    w2ring = Ring(b, [abf.alloc(4 * 4 * 128) for _ in range(2)], name=f"m2{tag}")
    shring = Ring(b, [abf.alloc(DCH * 128) for _ in range(3)], name=f"ms{tag}")
    gbrow = af32.alloc(64)
    gbb = af32.alloc(64)
    wd = af32.alloc(8 * 64)
    wdT = af32.alloc(8 * 128)
    wdrow = af32.alloc(TOK)
    sc_ = af32.alloc(64)
    sb2 = af32.alloc(64)
    mk = af32.alloc(64)
    wk64 = af32.alloc(64)
    gs = af32.alloc(8)
    wk8 = af32.alloc(8)
    eq8 = af32.alloc(8)
    cols = af32.alloc(8)
    m1c, m2c, t4c, t6c, ssc = (cols[:, i:i + 1] for i in range(5))

    dmo = b.dsem(f"moe{tag}")
    gw_tok = b.dma("g", dmo, gw.rearrange("p (dc e) -> p dc e", dc=DCH),
                   C["moe_gw"].ap().rearrange("(dc p) e -> p dc e", p=128))
    gb_tok = b.dma("g", dmo, gbrow[0:1, 0:64], C["moe_gb"][:, :])
    bk = banks.acquire()
    b.wait("t", gb_tok)
    mt = b.op("t", lambda e, bk=bk: e.matmul(pb[bk][:, 0:64], C["ones1x128_f32"][:, :], gbrow[0:1, 0:64], start=True, stop=True))
    b.wait("v", mt)
    gbb_tok = b.op("v", lambda e, bk=bk: e.tensor_copy(gbb, pb[bk][:, 0:64]))
    banks.release(bk, gbb_tok)

    def vnext():
        b.wait("v", b.tok("v"))

    rstate = {"wdT_tok": None}

    def do_routing_tile(tt_):
        wdT_tok = rstate["wdT_tok"]
        bk = banks.acquire()
        b.wait("t", ln_tok)
        b.wait("t", gw_tok)
        gwv = gw.rearrange("p (dc e) -> p dc e", dc=DCH)
        for dc in range(DCH):
            b.op("t", lambda e, tt_=tt_, dc=dc, bk=bk, gwv=gwv: e.matmul(
                pb[bk][:, 0:64], cpart(xn, dc, tt_ * 128, 128), gwv[:, dc, :],
                start=(dc == 0), stop=(dc == DCH - 1)), inc=(dc == DCH - 1))
        lg_tok = b.tok("t")
        b.wait("s", lg_tok)
        b.wait("s", wdT_tok)
        st = b.op("s", lambda e, bk=bk: e.activation(sc_, pb[bk][:, 0:64], AF.Sigmoid))
        banks.release(bk, st)
        b.wait("v", st)
        b.wait("v", gbb_tok)
        b.wait("v", wdT_tok)
        b.op("v", lambda e: e.tensor_tensor(sb2, sc_, gbb, ALU.add))
        vnext()
        for g in range(8):
            sbg = sb2[:, g * 8:(g + 1) * 8]
            b.op("v", lambda e, sbg=sbg: e.tensor_reduce(m1c, sbg, AX, ALU.max))
            vnext()
            b.op("v", lambda e, sbg=sbg: e.tensor_scalar(eq8, sbg, m1c, None, ALU.is_equal))
            vnext()
            b.op("v", lambda e, sbg=sbg: e.scalar_tensor_tensor(wk8, eq8, -BIG, sbg, ALU.mult, ALU.add))
            vnext()
            b.op("v", lambda e: e.tensor_reduce(m2c, wk8, AX, ALU.max))
            vnext()
            b.op("v", lambda e, g=g: e.tensor_tensor(gs[:, g:g + 1], m1c, m2c, ALU.add))
            vnext()
        b.op("v", lambda e: e.tensor_copy(wk8, gs))
        for i in range(3):
            vnext()
            b.op("v", lambda e: e.tensor_reduce(m1c, wk8, AX, ALU.max))
            vnext()
            b.op("v", lambda e: e.tensor_scalar(eq8, wk8, m1c, None, ALU.is_equal))
            vnext()
            b.op("v", lambda e: e.scalar_tensor_tensor(wk8, eq8, -BIG, wk8, ALU.mult, ALU.add))
        vnext()
        b.op("v", lambda e: e.tensor_reduce(t4c, wk8, AX, ALU.max))
        vnext()
        b.op("v", lambda e: e.tensor_scalar(gs, gs, t4c, None, ALU.is_ge))
        vnext()
        b.op("v", lambda e: e.tensor_scalar(wk8, gs, BIG, -BIG, ALU.mult, ALU.add))
        vnext()
        b.op("v", lambda e: e.tensor_tensor(
            mk.rearrange("p (g e) -> p g e", g=8), sb2.rearrange("p (g e) -> p g e", g=8),
            wk8.unsqueeze(2).broadcast_to([128, 8, 8]), ALU.add))
        vnext()
        b.op("v", lambda e: e.tensor_copy(wk64, mk))
        for i in range(5):
            vnext()
            b.op("v", lambda e: e.tensor_reduce(t6c, wk64, AX, ALU.max))
            vnext()
            b.op("v", lambda e: e.tensor_scalar(sb2, wk64, t6c, None, ALU.is_equal))
            vnext()
            b.op("v", lambda e: e.scalar_tensor_tensor(wk64, sb2, -BIG, wk64, ALU.mult, ALU.add))
        vnext()
        b.op("v", lambda e: e.tensor_reduce(t6c, wk64, AX, ALU.max))
        vnext()
        b.op("v", lambda e: e.tensor_scalar(mk, mk, t6c, None, ALU.is_ge))
        vnext()
        b.op("v", lambda e: e.tensor_tensor(mk, sc_, mk, ALU.mult))
        vnext()
        b.op("v", lambda e: e.tensor_reduce(ssc, mk, AX, ALU.add))
        vnext()
        b.op("v", lambda e: e.tensor_scalar(ssc, ssc, 1.0, 1e-20, ALU.mult, ALU.add))
        vnext()
        b.op("v", lambda e: e.reciprocal(ssc, ssc))
        vnext()
        b.op("v", lambda e: e.tensor_scalar(ssc, ssc, 2.5, None, ALU.mult))
        vnext()
        wd_tok = b.op("v", lambda e, tt_=tt_: e.tensor_scalar(wd[:, tt_ * 64:(tt_ + 1) * 64], mk, ssc, None, ALU.mult))
        bk = banks.acquire()
        b.wait("t", wd_tok)
        mt = b.op("t", lambda e, tt_=tt_, bk=bk: e.transpose(
            pb[bk][0:64, 0:128], wd[:, tt_ * 64:(tt_ + 1) * 64], C["idn_f32"][:, :]))
        b.wait("v", mt)
        wdT_tok = b.op("v", lambda e, tt_=tt_, bk=bk: e.tensor_copy(
            wdT[0:64, tt_ * 128:(tt_ + 1) * 128], pb[bk][0:64, 0:128]))
        banks.release(bk, wdT_tok)
        rstate["wdT_tok"] = wdT_tok

    upv = C["moe_shu"].ap().rearrange("(dc p) f -> p dc f", p=128)
    dnv = C["moe_shd"].ap().rearrange("(sc p) f -> p sc f", p=128)
    rsc_gate = [None, None]
    for qt in range(4):
        for sc2 in range(DCH):
            j = shring.load("g", upv[:, :, sc2 * 128:(sc2 + 1) * 128],
                            dstview=lambda bb: bb.rearrange("p (dc f) -> p dc f", dc=DCH))
            bk = banks.acquire()
            b.wait("t", shring.loaded[j])
            b.wait("t", ln_tok)
            wv = shring.bufs[j].rearrange("p (dc f) -> p dc f", dc=DCH)
            for dc in range(DCH):
                b.op("t", lambda e, wv=wv, dc=dc, qt=qt, bk=bk: e.matmul(
                    pb[bk][:, 0:QH], wv[:, dc, :], cpart(xn, dc, qt * QH, QH),
                    start=(dc == 0), stop=(dc == DCH - 1)), inc=(dc == DCH - 1))
            m_tok = b.tok("t")
            shring.used(j, m_tok)
            b.wait("s", m_tok)
            b.wait("s", rsc_gate[sc2 % 2])
            st = b.op("s", lambda e, sc2=sc2, bk=bk: e.activation(rsc[sc2 % 2][:, 0:QH], pb[bk][:, 0:QH], AF.Relu))
            b.wait("v", st)
            vt = b.op("v", lambda e, sc2=sc2, bk=bk: e.tensor_tensor(
                sh_hid[:, sc2 * QH:(sc2 + 1) * QH], pb[bk][:, 0:QH], rsc[sc2 % 2][:, 0:QH], ALU.mult))
            rsc_gate[sc2 % 2] = vt
            banks.release(bk, vt)
            if qt == 0 and sc2 < 8:
                do_routing_tile(sc2)
        hid_tok = b.tok("v")
        for fc in range(DCH):
            j = shring.load("g", dnv[:, :, fc * 128:(fc + 1) * 128],
                            dstview=lambda bb: bb.rearrange("p (sc f) -> p sc f", sc=DCH))
            bk = banks.acquire()
            b.wait("t", shring.loaded[j])
            b.wait("t", hid_tok)
            wv = shring.bufs[j].rearrange("p (sc f) -> p sc f", sc=DCH)
            for sc2 in range(DCH):
                b.op("t", lambda e, wv=wv, sc2=sc2, bk=bk: e.matmul(
                    pb[bk][:, 0:QH], wv[:, sc2, :], sh_hid[:, sc2 * QH:(sc2 + 1) * QH],
                    start=(sc2 == 0), stop=(sc2 == DCH - 1)), inc=(sc2 == DCH - 1))
            m_tok = b.tok("t")
            shring.used(j, m_tok)
            b.wait("v", m_tok)
            vt = b.op("v", lambda e, fc=fc, qt=qt, bk=bk: e.scalar_tensor_tensor(
                cpart(xr, fc, qt * QH, QH), pb[bk][:, 0:QH], 0.5, cpart(xr, fc, qt * QH, QH),
                ALU.mult, ALU.add))
            banks.release(bk, vt)

    w1v = C["moe_w1"].ap().rearrange("E (dc p) i -> p E dc i", p=128)
    w2v = C["moe_w2"].ap().rearrange("E (ic p) f -> p E ic f", p=128)
    down_tok = None
    wdrow_tok = None
    dwr = b.dsem(f"mwd{tag}")
    for grp in range(16):
        for ei in range(4):
            ex = grp * 4 + ei
            b.wait("y", rstate["wdT_tok"])
            b.wait("y", wdrow_tok)
            rtok = b.dma("y", dwr, wdrow[0:1, :], wdT[ex:ex + 1, :])
            wdb_tok = None
            for hf in (0, 1):
                bk = banks.acquire()
                b.wait("t", rtok)
                mt = b.op("t", lambda e, hf=hf, bk=bk: e.matmul(
                    pb[bk][:, :], C["ones1x128_f32"][:, :], wdrow[0:1, hf * H:(hf + 1) * H], start=True, stop=True))
                b.wait("v", mt)
                b.wait("v", b.tok("v"))
                wdb_tok = b.op("v", lambda e, hf=hf, bk=bk, ex=ex: e.tensor_copy(
                    wdb[ex % 2][:, hf * H:(hf + 1) * H], pb[bk][:, :]))
                banks.release(bk, wdb_tok)
            wdrow_tok = wdb_tok
            for ic in range(4):
                j = w1ring.load("g", w1v[:, ex, :, ic * 128:(ic + 1) * 128],
                                dstview=lambda bb: bb.rearrange("p (dc i) -> p dc i", dc=DCH))
                wv = w1ring.bufs[j].rearrange("p (dc i) -> p dc i", dc=DCH)
                for hf in (0, 1):
                    bk = banks.acquire()
                    b.wait("t", w1ring.loaded[j])
                    for dc in range(DCH):
                        b.op("t", lambda e, wv=wv, dc=dc, hf=hf, bk=bk: e.matmul(
                            pb[bk][:, :], wv[:, dc, :], cpart(xn, dc, hf * H, H),
                            start=(dc == 0), stop=(dc == DCH - 1)), inc=(dc == DCH - 1))
                    m_tok = b.tok("t")
                    w1ring.used(j, m_tok)
                    b.wait("s", m_tok)
                    b.wait("s", rsc_gate[hf])
                    st = b.op("s", lambda e, hf=hf, bk=bk: e.activation(rsc[hf][:, :], pb[bk][:, :], AF.Relu))
                    b.wait("v", st)
                    b.wait("v", wdb_tok)
                    b.wait("v", down_tok)
                    rsc_gate[hf] = b.op("v", lambda e, hf=hf, bk=bk: e.tensor_tensor(tsc2[hf][:, :], pb[bk][:, :], rsc[hf][:, :], ALU.mult))
                    vnext()
                    vt = b.op("v", lambda e, hf=hf, ei=ei, ic=ic, ex=ex: e.tensor_tensor(
                        hg[ei][:, ic * TOK + hf * H:ic * TOK + (hf + 1) * H],
                        tsc2[hf][:, :], wdb[ex % 2][:, hf * H:(hf + 1) * H], ALU.mult))
                    banks.release(bk, vt)
        up_tok = b.tok("v")
        for fc in range(DCH):
            j = w2ring.load("g", w2v[:, grp * 4:(grp + 1) * 4, :, fc * 128:(fc + 1) * 128],
                            dstview=lambda bb: bb.rearrange("p (E ic f) -> p E ic f", E=4, ic=4))
            wv = w2ring.bufs[j].rearrange("p (E ic f) -> p E ic f", E=4, ic=4)
            for hf in (0, 1):
                bk = banks.acquire()
                b.wait("t", w2ring.loaded[j])
                b.wait("t", up_tok)
                n = 0
                for ei in range(4):
                    for ic in range(4):
                        n += 1
                        b.op("t", lambda e, wv=wv, ei=ei, ic=ic, hf=hf, bk=bk, n=n: e.matmul(
                            pb[bk][:, :], wv[:, ei, ic, :],
                            hg[ei][:, ic * TOK + hf * H:ic * TOK + (hf + 1) * H],
                            start=(n == 1), stop=(n == 16)), inc=(n == 16))
                m_tok = b.tok("t")
                w2ring.used(j, m_tok)
                b.wait("v", m_tok)
                vt = b.op("v", lambda e, fc=fc, hf=hf, bk=bk: e.scalar_tensor_tensor(
                    cpart(xr, fc, hf * H, H), pb[bk][:, :], 0.5, cpart(xr, fc, hf * H, H),
                    ALU.mult, ALU.add))
                banks.release(bk, vt)
        down_tok = b.tok("t")
    abf.release(); af32.release()
    C["xres_tok"] = b.tok("v")
    return C["xres_tok"]


def emit_attn(b, C):
    xr = C["x_res"]
    pb, banks = C["pb"], C["banks"]
    abf, af32 = C["abf"], C["af32"]
    abf.mark(); af32.mark()
    qbuf = abf.alloc(DCH * TOK)
    wring = Ring(b, [abf.alloc(DCH * 128) for _ in range(4)], name="attw")
    abf.mark()
    xn = abf.alloc(DCH * TOK)
    ln_tok = emit_layernorm(b, C, xr, xn, C["xres_tok"])
    b.soft_fence()
    vfm = abf.alloc(DCH * TOK)
    ksc = [abf.alloc(H) for _ in range(2)]
    vsc = [abf.alloc(128) for _ in range(2)]
    dak = [b.dsem("atk0"), b.dsem("atk1")]
    dav = [b.dsem("atv0"), b.dsem("atv1")]
    wqv = C["at_wq"].ap().rearrange("(dc p) f -> p dc f", p=128)
    wkv = C["at_wk"].ap().rearrange("(dc p) f -> p dc f", p=128)
    wvv = C["at_wv"].ap().rearrange("(dc p) f -> p dc f", p=128)
    wov = C["at_wo"].ap().rearrange("(dc p) f -> p dc f", p=128)
    kbv = C["at_kb"].ap().rearrange("(c p) t -> p c t", p=128)
    vbv = C["at_vb"].ap().rearrange("(c p) f -> c p f", p=128)
    kst = [None, None]
    for fo in range(DCH):
        def ev_k(hf, bk, mm_tok, fo=fo):
            b.wait("v", mm_tok)
            b.wait("v", kst[hf])
            vt = b.op("v", lambda e, hf=hf, bk=bk, fo=fo: e.tensor_scalar(
                ksc[hf][:, :], pb[bk][:, :], C["at_bk"][:, fo:fo + 1], None, ALU.add))
            b.wait("y", vt)
            kst[hf] = b.dma("y", dak[hf], kbv[:, fo, hf * H:(hf + 1) * H], ksc[hf][:, :])
            return vt
        emit_proj_accum(b, C, wkv[:, :, fo * 128:(fo + 1) * 128], wring, xn, ln_tok, DCH, ev_k)
    for ds_ in dak:
        b.wait("g", (ds_.h, ds_.cnt))
    kg_tok = b.op("g", lambda e: e.collective_compute(
        "AllGather", ALU.bypass, replica_groups=RG,
        ins=[C["at_kb"].ap()], outs=[C["at_kg"].ap()]))
    vdma_gate = [None, None]
    for fo in range(DCH):
        def ev_q(hf, bk, mm_tok, fo=fo):
            b.wait("v", mm_tok)
            return b.op("v", lambda e, hf=hf, bk=bk, fo=fo: e.tensor_scalar(
                cpart(qbuf, fo, hf * H, H), pb[bk][:, :], C["at_bq"][:, fo:fo + 1], None, ALU.add))
        emit_proj_accum(b, C, wqv[:, :, fo * 128:(fo + 1) * 128], wring, xn, ln_tok, DCH, ev_q)
        vtk = [None]
        def ev_v(hf, bk, mm_tok, fo=fo, vtk=vtk):
            b.wait("v", mm_tok)
            vtk[0] = b.op("v", lambda e, hf=hf, bk=bk, fo=fo: e.tensor_scalar(
                cpart(vfm, fo, hf * H, H), pb[bk][:, :], C["at_bv"][:, fo:fo + 1], None, ALU.add))
            return vtk[0]
        emit_proj_accum(b, C, wvv[:, :, fo * 128:(fo + 1) * 128], wring, xn, ln_tok, DCH, ev_v)
        for tc in range(8):
            pt = C["ptA"] if tc % 2 == 0 else C["ptB"]
            ptfree = "ptA_free" if tc % 2 == 0 else "ptB_free"
            b.wait("t", C.get(ptfree))
            b.wait("t", vtk[0])
            mt = b.op("t", lambda e, fo=fo, tc=tc, pt=pt: e.transpose(
                pt[:, 0:128], cpart(vfm, fo, tc * 128, 128), C["idn_bf"][:, :]))
            b.wait("v", mt)
            b.wait("v", vdma_gate[tc % 2])
            vt = b.op("v", lambda e, tc=tc, pt=pt: e.tensor_copy(vsc[tc % 2][:, :], pt[:, 0:128]))
            C[ptfree] = vt
            b.wait("y", vt)
            vdma_gate[tc % 2] = b.dma("y", dav[tc % 2], vbv[tc, :, fo * 128:(fo + 1) * 128], vsc[tc % 2][:, :])
    for ds_ in dav:
        b.wait("g", (ds_.h, ds_.cnt))
    cc_tok = b.op("g", lambda e: e.collective_compute(
        "AllGather", ALU.bypass, replica_groups=RG,
        ins=[C["at_vb"].ap()], outs=[C["at_vg"].ap()]))
    abf.release()
    b.soft_fence()
    rrow = af32.alloc(2 * H)
    bcs = af32.alloc(2 * H)
    feats = abf.alloc(DCH * TOK)
    khp = [abf.alloc(len(RG[0]) * TOK) for _ in range(2)]
    vsb = [abf.alloc(130) for _ in range(3)]
    Pt = [abf.alloc(4 * H) for _ in range(2)]
    dkh = [b.dsem("akh0"), b.dsem("akh1")]
    dvh = [b.dsem("avh0"), b.dsem("avh1"), b.dsem("avh2")]
    ngrp = len(RG[0])
    SEQ = ngrp * TOK
    NKC = SEQ // 128
    kgv = C["at_kg"].ap().rearrange("(r c p) t -> r c p t", p=128, c=DCH)
    vgv = C["at_vg"].ap().rearrange("(c p) f -> c p f", p=128)
    b.wait("g", kg_tok)
    b.wait("g", cc_tok)
    on_tok = None
    for sl_ in range(3):
        on_tok = b.op("g", lambda e, sl_=sl_: e.memset(
            vsb[sl_][:, 0:130].rearrange("p (a x) -> p a x", a=2)[:, :, 64:65], 1.0))
    b.wait("t", on_tok)
    banks.take(2)
    banks.take(3)
    banks.take(4)
    obank = [pb[2], pb[3], pb[4], C["pbc"]]
    ob_free = [None, None, None, None]
    kuse = [None, None]
    khp_tok = [None, None]
    vuse = [None, None, None]
    pt_use = [None, None]
    nrm_gate = [None, None]
    for hp in range(16):
        jj = hp % 2
        b.wait("g", kuse[jj])
        for r in range(ngrp):
            khp_tok[jj] = b.dma("g", dkh[jj], khp[jj][:, r * TOK:(r + 1) * TOK], kgv[r, hp, :, :])
        av_last = [None, None, None, None]
        for tkc in range(NKC):
            vj = tkc % 3
            b.wait("g", vuse[vj])
            vs_tok = b.dma("g", dvh[vj],
                           vsb[vj][:, 0:130].rearrange("p (a x) -> p a x", a=2)[:, :, 0:64],
                           vgv[tkc, :, hp * 128:(hp + 1) * 128].rearrange("p (a x) -> p a x", a=2))
            pj = tkc % 2
            b.wait("s", pt_use[pj])
            exp_toks = []
            for h01 in (0, 1):
                base = h01 * 64
                for hf in (0, 1):
                    ch = h01 * 2 + hf
                    bk = banks.acquire()
                    b.wait("t", khp_tok[jj])
                    b.wait("t", ln_tok)
                    mt = b.op("t", lambda e, jj=jj, base=base, tkc=tkc, hf=hf, hp=hp, bk=bk: e.matmul(
                        pb[bk][:, :], khp[jj][base:base + 64, tkc * 128:(tkc + 1) * 128],
                        cpart(qbuf, hp, hf * H, H)[base:base + 64, :], start=True, stop=True))
                    b.wait("s", mt)
                    st = b.op("s", lambda e, pj=pj, ch=ch, bk=bk: e.activation(
                        Pt[pj][:, ch * H:(ch + 1) * H], pb[bk][:, :], AF.Exp, scale=ASCALE, bias=C["neg16"]))
                    banks.release(bk, st)
                    exp_toks.append((ch, st))
            for ch, st in exp_toks:
                h01 = ch // 2
                b.wait("t", st)
                b.wait("t", vs_tok)
                if tkc == 0:
                    b.wait("t", ob_free[ch])
                amt = b.op("t", lambda e, pj=pj, ch=ch, h01=h01, tkc=tkc: e.matmul(
                    obank[ch][0:65, :], vsb[tkc % 3][:, h01 * 65:(h01 + 1) * 65],
                    Pt[pj][:, ch * H:(ch + 1) * H], start=(tkc == 0), stop=(tkc == NKC - 1)))
                av_last[ch] = amt
            vuse[vj] = av_last[3]
            pt_use[pj] = av_last[3]
        kuse[jj] = b.tok("t")
        for ch in range(4):
            h01, hf = ch // 2, ch % 2
            nb = ch % 2
            b.wait("v", av_last[ch])
            b.wait("v", nrm_gate[nb])
            b.op("v", lambda e, ch=ch, nb=nb: e.reciprocal(rrow[0:1, nb * H:(nb + 1) * H], obank[ch][64:65, :]))
            rr_tok = b.tok("v")
            bk = banks.acquire()
            b.wait("t", rr_tok)
            mt = b.op("t", lambda e, bk=bk, nb=nb: e.matmul(
                pb[bk][0:64, :], C["ones1x128_f32"][:, 0:64], rrow[0:1, nb * H:(nb + 1) * H], start=True, stop=True))
            b.wait("v", mt)
            b.op("v", lambda e, bk=bk, nb=nb: e.tensor_copy(bcs[0:64, nb * H:(nb + 1) * H], pb[bk][0:64, :]))
            b.wait("v", b.tok("v"))
            ft = b.op("v", lambda e, hp=hp, hf=hf, h01=h01, ch=ch, nb=nb: e.tensor_tensor(
                cpart(feats, hp, hf * H, H)[h01 * 64:h01 * 64 + 64, :], obank[ch][0:64, :],
                bcs[0:64, nb * H:(nb + 1) * H], ALU.mult))
            banks.release(bk, ft)
            ob_free[ch] = ft
            nrm_gate[nb] = ft
    feats_tok = b.tok("v")
    banks.untake(2)
    banks.untake(3)
    banks.untake(4)
    for fo in range(DCH):
        def ev_out(hf, bk, mm_tok, fo=fo):
            b.wait("v", mm_tok)
            return b.op("v", lambda e, hf=hf, bk=bk, fo=fo: e.scalar_tensor_tensor(
                cpart(xr, fo, hf * H, H), pb[bk][:, :], C["at_bo"][:, fo:fo + 1],
                ALU.add, ALU.add, ) if False else e.scalar_tensor_tensor(
                cpart(xr, fo, hf * H, H), pb[bk][:, :], C["at_bo"][:, fo:fo + 1],
                cpart(xr, fo, hf * H, H), ALU.add, ALU.add))
        emit_proj_accum(b, C, wov[:, :, fo * 128:(fo + 1) * 128], wring, feats, feats_tok, DCH, ev_out)
    abf.release(); af32.release()
    C["xres_tok"] = b.tok("v")
    return C["xres_tok"]



CONSTS = {
    "idn_bf": ([128, 128], BF16),
    "idn_f32": ([128, 128], F32),
    "ones128_bf": ([128, 1], BF16),
    "ones128_f32": ([128, 1], F32),
    "ones1x128_f32": ([1, 128], F32),
    "maskT_bf": ([64, 64], BF16),
}


def build_program(parts):
    b = Builder()
    C = {}
    xin_d = b.dram("xin", [D, TOK], F32, kind="ExternalInput")
    xout_d = b.dram("xout", [D, TOK], F32, kind="ExternalOutput")
    need_ffn0 = any(p in parts for p in ("ffn0a", "ffn0b"))
    need_gla = "gla" in parts
    wd_in = {}
    if need_ffn0:
        for n, shape in (("w1t", [D, HID]), ("w2t", [D, HID]), ("w3t", [HID, D])):
            wd_in[n] = b.dram(n, shape, F32, kind="ExternalInput")
        qsp = "Shared" if len(RG8[0]) > 1 else "Local"
        C["wq1_d"] = b.nc.dram_tensor("wq1", [D, HID], BF16, kind="Internal", addr_space=qsp)
        C["wq2_d"] = b.nc.dram_tensor("wq2", [D, HID], BF16, kind="Internal", addr_space=qsp)
        C["wq3_d"] = b.nc.dram_tensor("wq3", [HID, D], BF16, kind="Internal", addr_space=qsp)
    if need_gla:
        for n in ("gla_wq", "gla_wk", "gla_wv", "gla_wo"):
            C[n] = b.dram(n, [D, D], F32, kind="ExternalInput")
        C["gla_nw_d"] = b.dram("gla_nw", [128, DCH], F32, kind="ExternalInput")
        C["glaU_d"] = b.dram("glaU_b", [128, 16 * 128], BF16)
        C["glaUg_d"] = b.dram("glaU_g", [len(RG[0]) * 128, 16 * 128], BF16)
    need_moe = any(p in parts for p in ("moe1a", "moe1b"))
    need_attn = "attn1" in parts
    if need_moe:
        C["moe_gw"] = b.dram("moe_gw", [D, 64], F32, kind="ExternalInput")
        C["moe_gb"] = b.dram("moe_gb", [1, 64], F32, kind="ExternalInput")
        C["moe_w1"] = b.dram("moe_w1", [64, D, 512], F32, kind="ExternalInput")
        C["moe_w2"] = b.dram("moe_w2", [64, 512, D], F32, kind="ExternalInput")
        C["moe_shu"] = b.dram("moe_shu", [D, D], F32, kind="ExternalInput")
        C["moe_shd"] = b.dram("moe_shd", [D, D], F32, kind="ExternalInput")
    if need_attn:
        for n in ("at_wq", "at_wk", "at_wv", "at_wo"):
            C[n] = b.dram(n, [D, D], F32, kind="ExternalInput")
        for n in ("at_bq_d", "at_bk_d", "at_bv_d", "at_bo_d"):
            C[n] = b.dram(n, [128, DCH], F32, kind="ExternalInput")
        ng = len(RG[0])
        C["at_kb"] = b.dram("at_kb", [D, TOK], BF16)
        C["at_vb"] = b.dram("at_vb", [TOK, D], BF16)
        C["at_kg"] = b.dram("at_kg", [ng * D, TOK], BF16)
        C["at_vg"] = b.dram("at_vg", [ng * TOK, D], BF16)

    for n, (shape, dt) in CONSTS.items():
        C[n] = b.sb(n, shape, dt)
    C["x_res"] = b.sb("x_res", [128, DCH * TOK], F32)
    C["zeros64"] = b.sb("zeros64", [128, 64], F32)[:, :]
    if need_gla:
        C["gla_nw"] = b.sb("gla_nw_sb", [128, DCH], F32)[:, :]
    if need_attn:
        for n in ("at_bq", "at_bk", "at_bv", "at_bo"):
            C[n] = b.sb(n + "_sb", [128, DCH], F32)[:, :]
        C["neg16"] = b.sb("neg16", [128, 1], F32)[:, :]
    C["abf"] = Arena(b, "abf", 60 * 1024, BF16)
    C["af32"] = Arena(b, "af32", 5120, F32)
    if need_ffn0:
        C["q_tot"] = b.sb("q_tot", [128, 1], F32)[:, :]
        C["q_thr1"] = b.sb("q_thr1", [1, 1], F32)[:, :]
        C["q_thrc"] = b.sb("q_thrc", [128, 4], F32)[:, :]
        C["q_nthr"] = b.sb("q_nthr", [128, 4], F32)[:, :]
        C["q_ptot"] = b.sb("q_ptot", [1, 8], F32)[:, :]
        C["qar_in"] = b.dram("qar_in", [1, 8], F32)
        C["qar_out"] = b.dram("qar_out", [1, 8], F32)
        NQ = len(RG8[0])
        for ji, nel in enumerate((D * HID, D * HID, HID * D)):
            slq = nel // NQ
            nchq = (slq + 128 * QW - 1) // (128 * QW)
            C[f"qb{ji}_d"] = b.dram(f"qb{ji}", [128 * nchq, slq // (128 * nchq)], BF16)

    C["pb"] = [b.ps(f"pb{i}", [128, 512], F32) for i in range(5)]
    C["ptA"] = b.ps("ptA", [128, 512], BF16)
    C["ptB"] = b.ps("ptB", [128, 512], BF16)
    C["pbc"] = b.ps("pbc", [128, 512], F32)
    C["banks"] = Banks(b, C["pb"])

    din = b.dsem("init")
    for n in CONSTS:
        cd = b.dram("c_" + n, list(CONSTS[n][0]), CONSTS[n][1], kind="ExternalInput")
        b.dma("g", din, C[n][:, :], cd[:, :])
    if need_gla:
        b.dma("g", din, C["gla_nw"], C["gla_nw_d"][:, :])
    if need_attn:
        for n in ("at_bq", "at_bk", "at_bv", "at_bo"):
            b.dma("g", din, C[n], C[n + "_d"][:, :])
    init_tok = b.dma("g", din, C["x_res"].ap().rearrange("p (c t) -> p c t", c=DCH),
                     xin_d.ap().rearrange("(c p) t -> p c t", p=128))
    zm_tok = b.op("g", lambda e: e.memset(C["zeros64"], 0.0))
    if need_attn:
        zm_tok = b.op("g", lambda e: e.memset(C["neg16"], -ACONST))
    C["xres_tok"] = None
    for e in ENG_ATTR:
        b.wait(e, init_tok)
        b.wait(e, zm_tok)

    C["wq_tok"] = None
    if need_ffn0:
        b.phase("quant")
        C["abf"].reset(); C["af32"].reset()
        abf, af32 = C["abf"], C["af32"]
        abf.mark(); af32.mark()
        qs = {
            "ring": Ring(b, [af32.alloc(QW) for _ in range(2)], name="qring"),
            "asums": af32.alloc(256),
            "qa": abf.alloc(QW),
            "qb": abf.alloc(QW),
            "ptot": abf.alloc(8).bitcast(F32)[0:1, 0:8] if False else C["q_ptot"],
            "std": b.dsem("qstore"),
        }
        C["wq_tok"] = emit_quantize_coop(b, C, qs, [
            (wd_in["w1t"], C["wq1_d"]),
            (wd_in["w2t"], C["wq2_d"]),
            (wd_in["w3t"], C["wq3_d"]),
        ])
        abf.release(); af32.release()

    for p in parts:
        b.phase(p)
        b.dma_fence()
        C["abf"].reset()
        C["af32"].reset()
        if p in ("ffn0a", "ffn0b"):
            emit_ffn0(b, C, p)
        elif p == "gla":
            emit_gla(b, C)
        elif p in ("moe1a", "moe1b"):
            emit_moe(b, C, p)
        elif p == "attn1":
            emit_attn(b, C)
        else:
            raise NotImplementedError(p)

    b.phase("fin")
    b.wait("g", C["xres_tok"])
    dout = b.dsem("out")
    otok = b.dma("g", dout, xout_d.ap().rearrange("(c p) t -> p c t", p=128),
                 C["x_res"].ap().rearrange("p (c t) -> p c t", c=DCH))
    b.raw("g", lambda e, sem=otok[0], v=otok[1]: e.wait_ge(sem, v))
    return _finalize(b), ["xout"]



def _consts_np():
    return {
        "c_idn_bf": np.eye(128, dtype=ml_dtypes.bfloat16),
        "c_idn_f32": np.eye(128, dtype=np.float32),
        "c_ones128_bf": np.ones((128, 1), ml_dtypes.bfloat16),
        "c_ones128_f32": np.ones((128, 1), np.float32),
        "c_ones1x128_f32": np.ones((1, 128), np.float32),
        "c_maskT_bf": np.triu(np.ones((64, 64))).astype(ml_dtypes.bfloat16),
    }


_PROG_CACHE = {}


def _get_prog(parts):
    key = tuple(parts)
    if key not in _PROG_CACHE:
        _PROG_CACHE[key] = build_program(list(parts))
    return _PROG_CACHE[key]


def _weights_np(parts, inp):
    w = {}
    if any(p in parts for p in ("ffn0a", "ffn0b")):
        w["w1t"] = np.ascontiguousarray(inp["L0_ffn_w1"].T)
        w["w2t"] = np.ascontiguousarray(inp["L0_ffn_w2"].T)
        w["w3t"] = np.ascontiguousarray(inp["L0_ffn_w3"].T)
    if "gla" in parts:
        w["gla_wq"] = np.ascontiguousarray(inp["L0_attn_Wq"].T)
        w["gla_wk"] = np.ascontiguousarray(inp["L0_attn_Wk"].T)
        w["gla_wv"] = np.ascontiguousarray(inp["L0_attn_Wv"].T)
        w["gla_wo"] = np.ascontiguousarray(inp["L0_attn_Wo"].T)
        w["gla_nw"] = np.ascontiguousarray(inp["L0_attn_norm_w"].reshape(DCH, 128).T)
    if any(p in parts for p in ("moe1a", "moe1b")):
        w["moe_gw"] = np.ascontiguousarray(inp["L1_ffn_gate_w"].T)
        w["moe_gb"] = np.ascontiguousarray(inp["L1_ffn_gate_bias"].reshape(1, 64))
        w["moe_w1"] = np.ascontiguousarray(inp["L1_ffn_w1"].transpose(0, 2, 1))
        w["moe_w2"] = np.ascontiguousarray(inp["L1_ffn_w2"].transpose(0, 2, 1))
        w["moe_shu"] = np.ascontiguousarray(inp["L1_ffn_shared_up"].T)
        w["moe_shd"] = np.ascontiguousarray(inp["L1_ffn_shared_down"].T)
    if "attn1" in parts:
        for a, bnm in (("wq", "Wq"), ("wk", "Wk"), ("wv", "Wv"), ("wo", "Wo")):
            w["at_" + a] = np.ascontiguousarray(inp["L1_attn_" + bnm])
        for a, bnm in (("bq", "bq"), ("bk", "bk"), ("bv", "bv"), ("bo", "bo")):
            w["at_" + a + "_d"] = np.ascontiguousarray(inp["L1_attn_" + bnm].reshape(DCH, 128).T)
    return w


def run_parts(groups, inp):
    x = inp["L0_attn_x"]
    cur = []
    for c in range(NC):
        bb, p = c // 4, c % 4
        cur.append(np.ascontiguousarray(x[bb, p * TOK:(p + 1) * TOK, :].T))
    consts = _consts_np()
    for parts in groups:
        nc, outs = _get_prog(tuple(parts))
        w = _weights_np(parts, inp)
        in_maps = [dict(xin=cur[c], **w, **consts) for c in range(NC)]
        res = run_bass_kernel_spmd(nc, in_maps, core_ids=list(range(NC)))
        cur = [np.asarray(res.results[c]["xout"]) for c in range(NC)]
    out = np.empty((2, 4096, D), np.float32)
    for c in range(NC):
        bb, p = c // 4, c % 4
        out[bb, p * TOK:(p + 1) * TOK, :] = cur[c].T
    return out


ALL_PARTS = ["ffn0a", "gla", "ffn0b", "moe1a", "attn1", "moe1b"]
GROUPS = [ALL_PARTS]


def kernel(**inp):
    return run_parts(GROUPS, inp)
